# version 16
# baseline (speedup 1.0000x reference)
import contextlib
import numpy as np
import concourse.bass as bass
import concourse.mybir as mybir
from concourse.bass_utils import run_bass_kernel_spmd

F32 = mybir.dt.float32
BF16 = mybir.dt.bfloat16
AF = mybir.ActivationFunctionType
ALU = mybir.AluOpType
AX = mybir.AxisListType

T = 4096
D = 1024
NT = T // 128
DC = D // 128
KB = 1024

_STAGE = "full"
_DBG_SKIP = ()
_NG, _NH, _NQ, _Q0 = 4, 4, 32, 0
_FL_SKIP = ()
_F2C_SKIP = ()


class Sem:
    __slots__ = ("name", "n")

    def __init__(self, name):
        self.name = name
        self.n = 0


class Prog:
    ENGINES = ("tensor", "vector", "scalar", "sync", "gpsimd")

    def __init__(self):
        self.ops = {e: [] for e in self.ENGINES}
        self.sems = []
        self._waited = {e: {} for e in self.ENGINES}

    def sem(self, name):
        s = Sem(name)
        self.sems.append(s)
        return s

    def emit(self, eng, meth, *args, w=(), **kw):
        ww = []
        for s, v in w:
            if v > 0 and self._waited[eng].get(s, 0) < v:
                self._waited[eng][s] = v
                ww.append((s, v))
        self.ops[eng].append([meth, args, kw, tuple(ww), ()])

    def inc_last(self, eng, sem, k=1):
        sem.n += k
        op = self.ops[eng][-1]
        op[4] = op[4] + ((sem, k),)
        return sem.n

    def dma(self, eng, out, in_, sem, w=(), **kw):
        self.emit(eng, "dma_start", out, in_, w=w, **kw)
        return self.inc_last(eng, sem, 16)


def build(stage="full"):
    skip = _DBG_SKIP
    nc = bass.Bass("TRN2", target_bir_lowering=False, debug=False)

    x_d = nc.dram_tensor("x", [T, D], F32, kind="ExternalInput").ap()
    wkt_d = nc.dram_tensor("wkt", [D, D], F32, kind="ExternalInput").ap()
    wqt_d = nc.dram_tensor("wqt", [D, D], F32, kind="ExternalInput").ap()
    wvt_d = nc.dram_tensor("wvt", [D, D], F32, kind="ExternalInput").ap()
    wot_d = nc.dram_tensor("wot", [D, D], F32, kind="ExternalInput").ap()
    fwqt_d = nc.dram_tensor("fwqt", [D, D], F32, kind="ExternalInput").ap()
    fwkct_d = nc.dram_tensor("fwkct", [D, 256], F32, kind="ExternalInput").ap()
    fwvct_d = nc.dram_tensor("fwvct", [D, 256], F32, kind="ExternalInput").ap()
    fwot_d = nc.dram_tensor("fwot", [D, D], F32, kind="ExternalInput").ap()
    fwrt_d = nc.dram_tensor("fwrt", [D, 8], F32, kind="ExternalInput").ap()
    ews_d = nc.dram_tensor("ews", [27, D, D], F32, kind="ExternalInput").ap()
    g1_d = nc.dram_tensor("g1", [D], F32, kind="ExternalInput").ap()
    g2_d = nc.dram_tensor("g2", [D], F32, kind="ExternalInput").ap()
    eb_d = nc.dram_tensor("eb", [8], F32, kind="ExternalInput").ap()
    ident_d = nc.dram_tensor("ident", [128, 128], F32, kind="ExternalInput").ap()
    mask_d = nc.dram_tensor("mask", [128, 128], F32, kind="ExternalInput").ap()
    ones_d = nc.dram_tensor("ones", [1, 128], F32, kind="ExternalInput").ap()
    out_d = nc.dram_tensor("out", [T, D], F32, kind="ExternalOutput").ap()
    dbg_d = nc.dram_tensor("dbg", [128, NT * 4], F32, kind="ExternalOutput").ap() \
        if stage == "xn" else None

    xn_h = nc.dram_tensor("xn_h", [T, D], F32, kind="Internal").ap()
    s_h = nc.dram_tensor("s_h", [T, D], F32, kind="Internal").ap()
    x1_h = nc.dram_tensor("x1_h", [T, D], F32, kind="Internal").ap()
    z_h = nc.dram_tensor("z_h", [T, D], BF16, kind="Internal").ap()
    ewbf_h = nc.dram_tensor("ewbf", [27, 128, 8 * KB], BF16, kind="Internal").ap()

    def sb(name, shape, dtype, off):
        return nc.alloc_sbuf_tensor_at(name, shape, dtype, offset=off + 16 * KB + 128)

    xn_T = sb("xn_T", [128, 8 * T], BF16, 0)
    qt_f = sb("qt_f", [128, 8 * T], BF16, 0)
    xf_T = sb("xf_T", [128, 8 * T], BF16, 0)
    nx_T = sb("nx_T", [128, 8 * T], BF16, 64 * KB)
    zf_sb = sb("zf_sb", [128, NT * KB], BF16, 64 * KB)
    kt_g = sb("kt_g", [128, 2 * T], BF16, 64 * KB)
    qt_g = sb("qt_g", [128, 2 * T], BF16, 80 * KB)
    v_g = sb("v_g", [128, NT * 256], BF16, 96 * KB)
    zg_sb = sb("zg_sb", [128, NT * 256], BF16, 112 * KB)
    h3_sb = sb("h3_sb", [128, 8 * KB], BF16, 64 * KB)
    h_sb = sb("h_sb", [128, 8 * KB], BF16, 80 * KB)
    moe_sb = sb("moe_sb", [128, 8 * KB], F32, 96 * KB)

    xt_sb = sb("xt_sb", [128, 2 * KB], F32, 128 * KB)
    xnf_sb = sb("xnf_sb", [128, 2 * KB], F32, 136 * KB)
    xnb_sb = sb("xnb_sb", [128, 2 * KB], BF16, 144 * KB)

    wk_sb = sb("wk_sb", [128, 2 * KB], BF16, 128 * KB)
    wq_sb = sb("wq_sb", [128, 2 * KB], BF16, 132 * KB)
    wv_sb = sb("wv_sb", [128, 2 * KB], BF16, 136 * KB)
    p_sb = sb("p_sb", [128, 3 * KB], BF16, 140 * KB)
    pt_sb = sb("pt_sb", [128, 3 * KB], BF16, 146 * KB)
    stg_sb = sb("stg_sb", [128, 8 * KB], BF16, 152 * KB)

    wot_sb = sb("wot_sb", [128, 8 * KB], BF16, 168 * KB)
    zt_sb = sb("zt_sb", [128, 2 * KB], BF16, 184 * KB)
    zT_sb = sb("zT_sb", [128, 2 * KB], BF16, 188 * KB)
    sst_sb = sb("sst_sb", [128, 2 * KB], F32, 128 * KB)

    sqf_sb = sb("sqf_sb", [128, 2 * 512], F32, 136 * KB)
    rrow_sb = sb("rrow_sb", [1, 2 * 512], F32, 140 * KB)
    g2r_sb = sb("g2r_sb", [1, KB], F32, 160 * KB)
    ebr_sb = sb("ebr_sb", [1, 8], F32, 164 * KB)

    fwq_sb = sb("fwq_sb", [128, 8 * KB], BF16, 128 * KB)
    fwkc_sb = sb("fwkc_sb", [128, 2 * KB], BF16, 144 * KB)
    fwvc_sb = sb("fwvc_sb", [128, 2 * KB], BF16, 148 * KB)
    kct_sb = sb("kct_sb", [128, 2 * T], BF16, 152 * KB)
    vc_sb = sb("vc_sb", [128, NT * 256], BF16, 168 * KB)

    fwot_sb = sb("fwot_sb", [128, 8 * KB], BF16, 128 * KB)
    zft_sb = sb("zft_sb", [128, 2 * KB], BF16, 144 * KB)
    zfT_sb = sb("zfT_sb", [128, 2 * KB], BF16, 148 * KB)
    xnl_sb = sb("xnl_sb", [128, 2 * KB], F32, 152 * KB)
    x1t_sb = sb("x1t_sb", [128, 2 * KB], F32, 160 * KB)
    xff_sb = sb("xff_sb", [128, 2 * KB], F32, 168 * KB)
    xfTf_sb = sb("xfTf_sb", [128, KB], F32, 176 * KB)

    wring = sb("wring", [128, 3 * 8 * KB], BF16, 128 * KB)
    xls_sb = sb("xls_sb", [128, KB], F32, 176 * KB)
    sls_sb = sb("sls_sb", [128, KB], F32, 180 * KB)
    x1l_sb = sb("x1l_sb", [128, KB], F32, 184 * KB)
    gl_sb = sb("gl_sb", [128, KB], F32, 188 * KB)
    tmps_sb = sb("tmps_sb", [128, 512], BF16, 192 * KB)
    outt_sb = sb("outt_sb", [128, KB], F32, 193 * KB)

    sqd_sb = sb("sqd_sb", [128, KB], BF16, 197 * KB)
    id_bf = sb("id_bf", [128, 128], BF16, 199 * KB)
    idf_sb = sb("idf_sb", [128, 128], F32, 199 * KB + 256)
    mask_sb = sb("mask_sb", [128, 128], F32, 199 * KB + 768)
    g2b_sb = sb("g2b_sb", [128, KB], F32, 200 * KB + 256)
    cmb_sb = sb("cmb_sb", [128, NT * 8], F32, 204 * KB + 256)
    onesr_sb = sb("onesr_sb", [1, 128], F32, 205 * KB + 256)
    onesc_sb = sb("onesc_sb", [128, 1], F32, 205 * KB + 768)
    g1_sb = sb("g1_sb", [128, 8], F32, 205 * KB + 800)
    wrt_sb = sb("wrt_sb", [128, 64], F32, 205 * KB + 832)
    ebb_sb = sb("ebb_sb", [128, 8], F32, 205 * KB + 1088)
    stoff = 206 * KB + 256
    nm_sb = sb("nm_sb", [128, 1], F32, stoff + 0)
    nm2_sb = sb("nm2_sb", [128, 1], F32, stoff + 32)
    pm_sb = sb("pm_sb", [128, 1], F32, stoff + 64)
    a0_sb = sb("a0_sb", [128, 1], F32, stoff + 96)
    l0_sb = sb("l0_sb", [128, 1], F32, stoff + 128)
    l1_sb = sb("l1_sb", [128, 1], F32, stoff + 160)
    rl_sb = sb("rl_sb", [128, 1], F32, stoff + 192)
    ssum_sb = sb("ssum_sb", [128, 1], F32, stoff + 224)
    osb_sb = sb("osb_sb", [128, 256], F32, stoff + 256)
    lg_sb = sb("lg_sb", [128, 8], F32, stoff + 1280)
    rwu_sb = sb("rwu_sb", [128, 8], F32, stoff + 1312)
    rw_sb = sb("rw_sb", [128, 8], F32, stoff + 1344)
    eq1_sb = sb("eq1_sb", [128, 8], F32, stoff + 1376)
    eq2_sb = sb("eq2_sb", [128, 8], F32, stoff + 1408)
    rw2_sb = sb("rw2_sb", [128, 8], F32, stoff + 1440)
    e12_sb = sb("e12_sb", [128, 2], F32, stoff + 1472)
    w12_sb = sb("w12_sb", [128, 2], F32, stoff + 1504)
    den_sb = sb("den_sb", [128, 1], F32, stoff + 1536)
    m1_sb = sb("m1_sb", [128, 1], F32, stoff + 1568)
    m2_sb = sb("m2_sb", [128, 1], F32, stoff + 1600)
    l2_sb = sb("l2_sb", [128, 1], F32, stoff + 1632)
    eps5_sb = sb("eps5_sb", [128, 1], F32, 205 * KB + 1152)
    eps6_sb = sb("eps6_sb", [128, 1], F32, 205 * KB + 1184)
    dbg_sb = sb("dbg_sb", [128, NT * 4], F32, 188 * KB) if stage == "xn" else None

    S_ps = nc.alloc_psum_tensor("S_ps", [128, 4 * 512], F32)
    TRA_ps = nc.alloc_psum_tensor("TRA_ps", [128, 1024], BF16)
    TRB_ps = nc.alloc_psum_tensor("TRB_ps", [128, 1024], BF16)
    AUX_ps = nc.alloc_psum_tensor("AUX_ps", [128, 512], F32)
    SPR_ps = nc.alloc_psum_tensor("SPR_ps", [128, 512], F32)
    TR_ps = TRA_ps

    def Sb(i):
        return S_ps[:, i * 512:(i + 1) * 512]

    p = Prog()
    pe, ve, se, sq, gq = (p.sem(n) for n in ("pe", "ve", "se", "sq", "gq"))
    E = p.emit
    IL = p.inc_last

    gq_id = p.dma("gpsimd", id_bf[:], ident_d, gq)
    sq_mask = p.dma("sync", mask_sb[:], mask_d, sq)
    sq_idf = p.dma("sync", idf_sb[:], ident_d, sq)
    sq_ones = p.dma("sync", onesr_sb[:], ones_d, sq)
    sq_g1 = p.dma("sync", g1_sb[:], g1_d.rearrange("(c p) -> p c", p=128), sq,
                  allow_slow_non_contiguous=True)
    sq_wrt = p.dma("sync", wrt_sb[:].rearrange("p (c n) -> p c n", c=8),
                   fwrt_d.rearrange("(c p) n -> p c n", p=128), sq)
    sq_g2r = p.dma("sync", g2r_sb[:], g2_d.rearrange("(a n) -> a n", a=1), sq)
    sq_ebr = p.dma("sync", ebr_sb[:], eb_d.rearrange("(a n) -> a n", a=1), sq)
    E("vector", "memset", onesc_sb[:], 1.0)
    E("vector", "memset", eps5_sb[:], 1e-5)
    E("vector", "memset", eps6_sb[:], 1e-6)
    ve_onesc = IL("vector", ve)

    skip = _DBG_SKIP
    if "bcast" not in skip:
        E("tensor", "matmul", Sb(0), onesr_sb[:], g2r_sb[:, 0:512], start=True, stop=True,
          w=((sq, sq_g2r), (sq, sq_ones)))
    if "bcast" not in skip:
        E("tensor", "matmul", Sb(1), onesr_sb[:], g2r_sb[:, 512:1024], start=True, stop=True)
        pe_g2b = IL("tensor", pe)
        E("tensor", "matmul", AUX_ps[:, 0:8], onesr_sb[:], ebr_sb[:], start=True, stop=True,
          w=((sq, sq_ebr),))
        pe_ebb = IL("tensor", pe)
        E("vector", "tensor_copy", g2b_sb[:, 0:512], Sb(0), w=((pe, pe_g2b),))
        E("vector", "tensor_copy", g2b_sb[:, 512:1024], Sb(1))
        E("vector", "tensor_copy", ebb_sb[:], AUX_ps[:, 0:8], w=((pe, pe_ebb),))
    else:
        E("vector", "engine_nop")
    ve_bcast = IL("vector", ve)

    fst = {
        "s_free": ((ve, ve_bcast),),
        "aux": ((ve, ve_bcast),),
        "tr": [((ve, 0),), ((ve, 0),)],
        "p_pe": 0,
        "pt_pe": 0,
    }
    bank_free = {i: ((ve, ve_bcast),) for i in range(4)}

    xt_free = [(), ()]
    xnf_free = [(), ()]
    xnb_free = [(), ()]
    se_sq_prev = 0
    for tt in range(NT):
        b = tt % 2
        xt = xt_sb[:, b * KB:(b + 1) * KB]
        xnf = xnf_sb[:, b * KB:(b + 1) * KB]
        xnb = xnb_sb[:, b * KB:(b + 1) * KB]
        xv = p.dma("sync", xt, x_d[tt * 128:(tt + 1) * 128, :], sq, w=xt_free[b])
        E("vector", "tensor_reduce", nm_sb[:], xt, AX.X, ALU.add, negate=True,
          w=((sq, xv), (se, se_sq_prev)))
        v = IL("vector", ve)
        E("vector", "tensor_scalar_mul", nm_sb[:], nm_sb[:], 1.0 / D, w=((ve, v),))
        ve_nm = IL("vector", ve)
        E("scalar", "activation", sqd_sb[:], xt, AF.Square, bias=nm_sb[:],
          accum_out=ssum_sb[:], w=((ve, ve_nm),))
        v = IL("scalar", se)
        se_sq_prev = v
        E("scalar", "activation", ssum_sb[:], ssum_sb[:], AF.Ln, scale=1.0 / D,
          bias=eps5_sb[:], w=((se, v), (ve, ve_onesc)))
        v = IL("scalar", se)
        E("scalar", "activation", rl_sb[:], ssum_sb[:], AF.Exp, scale=-0.5, w=((se, v),))
        se_st = IL("scalar", se)
        E("vector", "tensor_scalar", xnf, xt, nm_sb[:], rl_sb[:], ALU.add, ALU.mult,
          w=((se, se_st),) + xnf_free[b])
        ve_xnf = IL("vector", ve)
        xt_free[b] = ((ve, ve_xnf),)
        sv = p.dma("sync", xn_h[tt * 128:(tt + 1) * 128, :], xnf, sq, w=((ve, ve_xnf),))
        if "cast" not in skip:
            E("vector", "tensor_copy", xnb, xnf, w=((ve, ve_xnf),) + xnb_free[b])
            ve_cast = IL("vector", ve)
            xnf_free[b] = ((ve, ve_cast), (sq, sv))
        else:
            ve_cast = 0
            xnf_free[b] = ((sq, sv),)
        if "tr" not in skip:
            for c in range(DC):
                E("tensor", "transpose", TR_ps[:, c * 128:(c + 1) * 128],
                  xnb[:, c * 128:(c + 1) * 128], id_bf[:],
                  w=((ve, ve_cast), (gq, gq_id)) + fst["tr"][0] + fst["tr"][1]
                    if c == 0 else ())
            pe_tr = IL("tensor", pe)
            xnb_free[b] = ((pe, pe_tr),)
            for c in range(DC):
                E("vector", "tensor_copy",
                  xn_T[:, c * T + tt * 128:c * T + tt * 128 + 128],
                  TR_ps[:, c * 128:(c + 1) * 128],
                  w=((pe, pe_tr),) if c == 0 else ())
            v = IL("vector", ve)
            fst["tr"] = [((ve, v),), ((ve, v),)]
        if stage == "xn":
            E("vector", "tensor_copy", dbg_sb[:, tt * 4:tt * 4 + 1], nm_sb[:],
              w=((ve, ve_xnf),))
            E("vector", "tensor_copy", dbg_sb[:, tt * 4 + 1:tt * 4 + 2], ssum_sb[:])
            E("vector", "tensor_copy", dbg_sb[:, tt * 4 + 2:tt * 4 + 3], rl_sb[:])
            E("vector", "tensor_copy", dbg_sb[:, tt * 4 + 3:tt * 4 + 4], eps5_sb[:])
            IL("vector", ve)
    sq_p0 = sq.n

    if stage == "xn":
        p.dma("sync", dbg_d, dbg_sb[:], sq, w=((ve, ve.n),))
        _copy_out(p, sq, xn_h, out_d, xls_sb, sq_p0, extra=__fence)
        return _close(nc, p)

    sq_conv = [0]
    def emit_conversions():
        stg_free = [(), ()]
        for m in range(27):
            for hh in range(2):
                b = (m * 2 + hh) % 2
                stg = stg_sb[:, b * 4 * KB:(b + 1) * 4 * KB]
                src = ews_d[m].rearrange("(c p) n -> p c n", p=128)[:, hh * 4:(hh + 1) * 4, :]
                gv = p.dma("gpsimd", stg[:].rearrange("p (c n) -> p c n", c=4), src, gq,
                           w=stg_free[b])
                sv = p.dma("sync", ewbf_h[m][:, hh * 4 * KB:(hh + 1) * 4 * KB], stg, sq,
                           w=((gq, gv),))
                stg_free[b] = ((sq, sv),)

    def fence():
        return ((pe, pe.n), (ve, ve.n), (se, se.n), (sq, sq.n))

    def mm_group(dst_ps, lhsT_list, rhs_list, extra_w=()):
        n = len(lhsT_list)
        for i, (lh, rh) in enumerate(zip(lhsT_list, rhs_list)):
            E("tensor", "matmul", dst_ps, lh, rh, start=(i == 0), stop=(i == n - 1),
              w=extra_w if i == 0 else ())
        return IL("tensor", pe)

    def proj_fm(out_sb, out_cols, w_sb, w_cols, act_T, extra_w=()):
        first = True
        for oc in range(out_cols):
            for ts in range(T // 512):
                bkk = (oc * (T // 512) + ts) % 4
                v = mm_group(
                    Sb(bkk),
                    [w_sb[:, c * w_cols + oc * 128:c * w_cols + oc * 128 + 128]
                     for c in range(DC)],
                    [act_T[:, c * T + ts * 512:c * T + ts * 512 + 512] for c in range(DC)],
                    extra_w=bank_free[bkk] + (extra_w if first else ()))
                first = False
                E("vector", "tensor_copy",
                  out_sb[:, oc * T + ts * 512:oc * T + ts * 512 + 512], Sb(bkk),
                  w=((pe, v),))
                bank_free[bkk] = ((ve, IL("vector", ve)),)

    def proj_tm_256(out_sb, w_sb, act_T, extra_w=()):
        first = True
        for tt in range(NT):
            bkk = tt % 4
            v = mm_group(
                S_ps[:, bkk * 512:bkk * 512 + 256],
                [act_T[:, c * T + tt * 128:c * T + tt * 128 + 128] for c in range(DC)],
                [w_sb[:, c * 256:(c + 1) * 256] for c in range(DC)],
                extra_w=bank_free[bkk] + (extra_w if first else ()))
            first = False
            E("vector", "tensor_copy", out_sb[:, tt * 256:(tt + 1) * 256],
              S_ps[:, bkk * 512:bkk * 512 + 256], w=((pe, v),))
            bank_free[bkk] = ((ve, IL("vector", ve)),)

    def flash(L, scale, causal, qls, k_row, av_rhs, dh, z_dst, z_w):
        if causal:
            two = L > 2048
            passes = [(0, min(L, 2048))] + ([(2048, L - 2048)] if two else [])
        else:
            two = True
            passes = [(0, 2048), (2048, 2048)]
        se_a0 = 0
        for pi, (off, Lp) in enumerate(passes):
            split = Lp > 1024
            qk_inc = []
            for hb in range(2 if split else 1):
                h0 = hb * 1024
                hl_len = min(1024, Lp - h0)
                for j in range(h0 // 512, (h0 + hl_len + 511) // 512):
                    n = min(512, Lp - j * 512)
                    krs = k_row(off + j * 512, n)
                    for ci, (ql, kr) in enumerate(zip(qls, krs)):
                        E("tensor", "matmul", S_ps[:, j * 512:j * 512 + n], ql, kr,
                          start=(ci == 0), stop=(ci == len(qls) - 1),
                          w=fst["s_free"] if (j == 0 and ci == 0) else ())
                qk_inc.append(IL("tensor", pe))
            pe_qk = qk_inc[-1]
            ve_mask = 0
            if causal and off + Lp == L:
                E("vector", "tensor_tensor", S_ps[:, Lp - 128:Lp], S_ps[:, Lp - 128:Lp],
                  mask_sb[:], ALU.add, w=((pe, pe_qk), (sq, sq_mask)))
                ve_mask = IL("vector", ve)
            if split:
                la_len = 1024
                subA = bass.AP(S_ps, 0, [[4 * 512, 128], [16, la_len // 16]])
                E("vector", "tensor_reduce", m1_sb[:], subA, AX.X, ALU.max, negate=True,
                  w=((pe, qk_inc[0]),))
                IL("vector", ve)
                subB = bass.AP(S_ps, 1024, [[4 * 512, 128], [16, (Lp - 1024) // 16]])
                E("vector", "tensor_reduce", nm_sb[:], subB, AX.X, ALU.max, negate=True,
                  w=((pe, pe_qk),) + (((ve, ve_mask),) if ve_mask else ()))
                vv = IL("vector", ve)
                E("vector", "tensor_scalar", nm_sb[:], nm_sb[:], m1_sb[:], scale,
                  ALU.min, ALU.mult, w=((ve, vv),))
                vv = IL("vector", ve)
            else:
                sub = bass.AP(S_ps, 0, [[4 * 512, 128], [16, Lp // 16]])
                E("vector", "tensor_reduce", nm_sb[:], sub, AX.X, ALU.max, negate=True,
                  w=((pe, pe_qk),) + (((ve, ve_mask),) if ve_mask else ()))
                vv = IL("vector", ve)
                E("vector", "tensor_scalar_mul", nm_sb[:], nm_sb[:], scale, w=((ve, vv),))
                vv = IL("vector", ve)
            if pi == 0:
                nm_use = nm_sb
                if two:
                    E("vector", "tensor_copy", nm2_sb[:], nm_sb[:], w=((ve, vv),))
                    E("vector", "tensor_scalar_mul", pm_sb[:], nm_sb[:], -1.0)
                    vv = IL("vector", ve)
            else:
                E("vector", "tensor_tensor", nm2_sb[:], nm2_sb[:], nm_sb[:], ALU.min,
                  w=((ve, vv),))
                vv = IL("vector", ve)
                E("scalar", "activation", a0_sb[:], nm2_sb[:], AF.Exp, bias=pm_sb[:],
                  w=((ve, vv),))
                se_a0 = IL("scalar", se)
                nm_use = nm2_sb
            lpa = l0_sb if pi == 0 else l1_sb
            nla = min(1024, Lp)
            E("scalar", "activation", p_sb[:, 0:nla], S_ps[:, 0:nla], AF.Exp,
              bias=nm_use[:], scale=scale, accum_out=lpa[:],
              w=((ve, vv), (pe, fst["p_pe"])))
            se_expA = IL("scalar", se)
            if split:
                E("scalar", "activation", p_sb[:, 1024:Lp], S_ps[:, 1024:Lp], AF.Exp,
                  bias=nm_use[:], scale=scale, accum_out=l2_sb[:])
                se_expB = IL("scalar", se)
                E("vector", "tensor_add", lpa[:], lpa[:], l2_sb[:], w=((se, se_expB),))
                IL("vector", ve)
            else:
                se_expB = se_expA
            fst["s_free"] = ((se, se_expB),)
            nch = Lp // 128
            cp_inc = []
            for grp in range(0, nch, 8):
                gend = min(grp + 8, nch)
                half = (grp // 8) % 2
                trb = TRA_ps if half == 0 else TRB_ps
                se_w = se_expA if grp == 0 else se_expB
                for c in range(grp, gend):
                    E("tensor", "transpose", trb[:, (c - grp) * 128:(c - grp + 1) * 128],
                      p_sb[:, c * 128:(c + 1) * 128], id_bf[:],
                      w=(((se, se_w),) + fst["tr"][half]) if c == grp else ())
                pe_tr = IL("tensor", pe)
                E("vector", "tensor_copy", pt_sb[:, grp * 128:gend * 128],
                  trb[:, 0:(gend - grp) * 128],
                  w=((pe, pe_tr),) + (((pe, fst["pt_pe"]),) if grp == 0 else ()))
                cp_inc.append(IL("vector", ve))
                fst["tr"][half] = ((ve, cp_inc[-1]),)
            fst["p_pe"] = pe.n
            kt0 = off // 128
            for c in range(nch):
                wv = ()
                if c == 0:
                    wv = ((ve, cp_inc[0]),) + fst["aux"]
                elif c == 8 and len(cp_inc) > 1:
                    wv = ((ve, cp_inc[1]),)
                E("tensor", "matmul", AUX_ps[:, 0:dh], pt_sb[:, c * 128:(c + 1) * 128],
                  av_rhs(kt0 + c), start=(c == 0), stop=(c == nch - 1), w=wv)
            pe_av = IL("tensor", pe)
            fst["pt_pe"] = pe_av
            if not two:
                E("vector", "reciprocal", rl_sb[:], l0_sb[:], w=((se, se_expA), (ve, ve.n)))
                vv = IL("vector", ve)
                E("vector", "tensor_scalar_mul", z_dst, AUX_ps[:, 0:dh], rl_sb[:],
                  w=((ve, vv), (pe, pe_av)) + z_w)
                fst["aux"] = ((ve, IL("vector", ve)),)
            elif pi == 0:
                E("vector", "tensor_copy", osb_sb[:, 0:dh], AUX_ps[:, 0:dh],
                  w=((pe, pe_av),))
                fst["aux"] = ((ve, IL("vector", ve)),)
            else:
                E("vector", "scalar_tensor_tensor", l0_sb[:], l0_sb[:], a0_sb[:],
                  l1_sb[:], ALU.mult, ALU.add, w=((se, se_a0), (ve, ve.n)))
                vv = IL("vector", ve)
                E("vector", "reciprocal", rl_sb[:], l0_sb[:], w=((ve, vv),))
                vv = IL("vector", ve)
                E("vector", "scalar_tensor_tensor", osb_sb[:, 0:dh], osb_sb[:, 0:dh],
                  a0_sb[:], AUX_ps[:, 0:dh], ALU.mult, ALU.add, w=((ve, vv), (pe, pe_av)))
                fst["aux"] = ((ve, IL("vector", ve)),)
                E("vector", "tensor_scalar_mul", z_dst, osb_sb[:, 0:dh], rl_sb[:],
                  w=((ve, ve.n),) + z_w)
                IL("vector", ve)

    sq_z = 0
    for g in range(_NG):
        wws = []
        fw = fence()
        for w_s, w_dd in ((wk_sb, wkt_d), (wq_sb, wqt_d), (wv_sb, wvt_d)):
            gv = p.dma("gpsimd", w_s[:].rearrange("p (c n) -> p c n", c=DC),
                       w_dd.rearrange("(c p) n -> p c n", p=128)[:, :, g * 256:(g + 1) * 256],
                       gq, w=fw)
            wws.append((gq, gv))
        if g == 0:
            emit_conversions()
            sq_conv[0] = sq.n
        for i in range(4):
            bank_free[i] = bank_free[i] + fst["s_free"]
        proj_fm(kt_g[:], 2, wk_sb[:], 256, xn_T[:], extra_w=(wws[0],))
        proj_fm(qt_g[:], 2, wq_sb[:], 256, xn_T[:], extra_w=(wws[1],))
        proj_tm_256(v_g[:], wv_sb[:], xn_T[:], extra_w=(wws[2],))
        fst["s_free"] = bank_free[0] + bank_free[1] + bank_free[2] + bank_free[3] + \
            fst["s_free"]
        for hl in range(_NH):
            pl, pb = hl // 2, 64 * (hl % 2)
            for qt in range(_Q0, min(NT, _NQ)):
                flash(L=(qt + 1) * 128, scale=0.125, causal=True,
                      qls=[qt_g[pb:pb + 64, pl * T + qt * 128:pl * T + qt * 128 + 128]],
                      k_row=lambda off, n, pl=pl, pb=pb:
                          [kt_g[pb:pb + 64, pl * T + off:pl * T + off + n]],
                      av_rhs=lambda kt, hl=hl:
                          v_g[:, kt * 256 + hl * 64:kt * 256 + hl * 64 + 64],
                      dh=64,
                      z_dst=zg_sb[:, qt * 256 + hl * 64:qt * 256 + hl * 64 + 64],
                      z_w=((sq, sq_z),))
        dstz = z_h.rearrange("(t p) n -> p t n", p=128)[:, :, g * 256:(g + 1) * 256]
        sq_z = p.dma("sync", dstz, zg_sb[:].rearrange("p (t n) -> p t n", n=256), sq,
                     w=((ve, ve.n),))

    if stage == "z":
        p.dma("gpsimd", out_d, z_h, gq, w=((sq, sq_z),))
        p.emit("gpsimd", "nop", w=((gq, gq.n),))
        return _close(nc, p)

    gv_wot = p.dma("gpsimd", wot_sb[:].rearrange("p (c n) -> p c n", c=DC),
                   wot_d.rearrange("(c p) n -> p c n", p=128), gq, w=fence())
    for i in range(4):
        bank_free[i] = bank_free[i] + fst["s_free"]
    zt_free = [(), ()]
    sst_free = [(), ()]
    for tt in range(NT):
        b = tt % 2
        zt = zt_sb[:, b * KB:(b + 1) * KB]
        zT = zT_sb[:, b * KB:(b + 1) * KB]
        sst = sst_sb[:, b * KB:(b + 1) * KB]
        zv = p.dma("sync", zt, z_h.rearrange("(t p) n -> p t n", p=128)[:, tt, :], sq,
                   w=zt_free[b] + ((sq, sq_z),))
        for c in range(DC):
            E("tensor", "transpose", TR_ps[:, c * 128:(c + 1) * 128],
              zt[:, c * 128:(c + 1) * 128], id_bf[:],
              w=(((sq, zv),) + fst["tr"][0] + fst["tr"][1]) if c == 0 else ())
        pe_tr = IL("tensor", pe)
        zt_free[b] = ((pe, pe_tr),)
        E("vector", "tensor_copy", zT[:].rearrange("p (c n) -> p c n", n=128),
          TR_ps[:].rearrange("p (c n) -> p c n", n=128), w=((pe, pe_tr),))
        ve_zT = IL("vector", ve)
        fst["tr"] = [((ve, ve_zT),), ((ve, ve_zT),)]
        for half in range(2):
            bk = (tt % 2) * 2 + half
            v = mm_group(Sb(bk),
                         [zT[:, c * 128:(c + 1) * 128] for c in range(DC)],
                         [wot_sb[:, c * KB + half * 512:c * KB + half * 512 + 512]
                          for c in range(DC)],
                         extra_w=((ve, ve_zT), (gq, gv_wot)) + bank_free[bk])
            E("vector", "tensor_copy", sst[:, half * 512:half * 512 + 512], Sb(bk),
              w=((pe, v),) + (sst_free[b] if half == 0 else ()))
            bank_free[bk] = ((ve, IL("vector", ve)),)
        sv = p.dma("sync", s_h[tt * 128:(tt + 1) * 128, :], sst, sq, w=((ve, ve.n),))
        sst_free[b] = ((sq, sv),)
    sq_p1c = sq.n

    if stage == "s":
        __fence = ((pe, pe.n), (ve, ve.n), (se, se.n))
        _copy_out(p, sq, s_h, out_d, xls_sb, sq_p1c, extra=__fence)
        return _close(nc, p)

    sqf_free = [(), ()]
    aux_free = fst["aux"]
    for ts in range(T // 512):
        for c in range(DC):
            b = c % 2
            sqf = sqf_sb[:, b * 512:(b + 1) * 512]
            E("scalar", "activation", sqf,
              xn_T[:, c * T + ts * 512:c * T + ts * 512 + 512], AF.Square,
              w=sqf_free[b])
            se_sq = IL("scalar", se)
            E("tensor", "matmul", AUX_ps[0:1, 0:512], onesc_sb[:], sqf,
              start=(c == 0), stop=(c == DC - 1),
              w=((se, se_sq), (ve, ve_onesc)) + (aux_free if c == 0 else ()))
            sqf_free[b] = ((pe, IL("tensor", pe)),)
        pe_ss = pe.n
        rrow = rrow_sb[:, (ts % 2) * 512:(ts % 2) * 512 + 512]
        E("scalar", "activation", rrow, AUX_ps[0:1, 0:512], AF.Ln, scale=1.0 / D,
          bias=eps6_sb[0:1, :], w=((pe, pe_ss), (ve, ve_onesc)))
        v = IL("scalar", se)
        E("scalar", "activation", rrow, rrow, AF.Exp, scale=-0.5, w=((se, v),))
        se_r = IL("scalar", se)
        aux_free = ((se, se_r),)
        bk = ts % 2
        E("tensor", "matmul", Sb(bk), onesr_sb[:], rrow, start=True, stop=True,
          w=((se, se_r),) + bank_free[bk])
        pe_bc = IL("tensor", pe)
        for c in range(DC):
            E("vector", "scalar_tensor_tensor",
              nx_T[:, c * T + ts * 512:c * T + ts * 512 + 512],
              xn_T[:, c * T + ts * 512:c * T + ts * 512 + 512],
              g1_sb[:, c:c + 1], Sb(bk), ALU.mult, ALU.mult,
              w=((pe, pe_bc), (sq, sq_g1)) if c == 0 else ())
        bank_free[bk] = ((ve, IL("vector", ve)),)
    fst["aux"] = aux_free
    ve_f1 = ve.n

    if stage == "f1":
        __fence = ((pe, pe.n), (ve, ve.n), (se, se.n))
        _copy_out(p, sq, xn_h, out_d, xls_sb, sq_p1c, extra=__fence)
        return _close(nc, p)

    gv_fwq = p.dma("gpsimd", fwq_sb[:].rearrange("p (c n) -> p c n", c=DC),
                   fwqt_d.rearrange("(c p) n -> p c n", p=128), gq, w=fence())
    gv_fwkc = p.dma("gpsimd", fwkc_sb[:].rearrange("p (c n) -> p c n", c=DC),
                    fwkct_d.rearrange("(c p) n -> p c n", p=128), gq)
    gv_fwvc = p.dma("gpsimd", fwvc_sb[:].rearrange("p (c n) -> p c n", c=DC),
                    fwvct_d.rearrange("(c p) n -> p c n", p=128), gq)
    proj_fm(qt_f[:], 8, fwq_sb[:], KB, nx_T[:], extra_w=((gq, gv_fwq), (ve, ve_f1)))
    proj_fm(kct_sb[:], 2, fwkc_sb[:], 256, nx_T[:],
            extra_w=((gq, gv_fwkc), (sq, sq_conv[0])))
    proj_tm_256(vc_sb[:], fwvc_sb[:], nx_T[:], extra_w=((gq, gv_fwvc),))
    fst["s_free"] = bank_free[0] + bank_free[1] + bank_free[2] + bank_free[3] + \
        fst["s_free"]

    if stage == "f2":
        __fence = ((pe, pe.n), (ve, ve.n), (se, se.n))
        _copy_out(p, sq, xn_h, out_d, xls_sb, sq_p1c, extra=__fence)
        return _close(nc, p)

    for hl in range(4):
        for qt in range(NT):
            flash(L=T, scale=1.0 / 16.0, causal=False,
                  qls=[qt_f[:, (2 * hl + j) * T + qt * 128:(2 * hl + j) * T + qt * 128 + 128]
                       for j in range(2)],
                  k_row=lambda off, n: [kct_sb[:, j * T + off:j * T + off + n]
                                        for j in range(2)],
                  av_rhs=lambda kt: vc_sb[:, kt * 256:(kt + 1) * 256],
                  dh=256,
                  z_dst=zf_sb[:, qt * KB + hl * 256:qt * KB + hl * 256 + 256],
                  z_w=())
    ve_f2b = ve.n
    pe_f2b = pe.n

    if stage == "f2b":
        __fence = ((pe, pe.n), (ve, ve.n), (se, se.n))
        _copy_out(p, sq, xn_h, out_d, xls_sb, sq_p1c, extra=__fence)
        return _close(nc, p)

    gv_fwot = p.dma("gpsimd", fwot_sb[:].rearrange("p (c n) -> p c n", c=DC),
                    fwot_d.rearrange("(c p) n -> p c n", p=128), gq, w=fence())
    for i in range(4):
        bank_free[i] = bank_free[i] + fst["s_free"]
    zft_free = [(), ()]
    x1_free = [(), ()]
    xn_free = [(), ()]
    xff_free = [(), ()]
    aux_r = fst["aux"]
    b23_free = ()
    for tt in range(NT):
        b = tt % 2
        zft = zft_sb[:, b * KB:(b + 1) * KB]
        zfT = zfT_sb[:, b * KB:(b + 1) * KB]
        xnl = xnl_sb[:, b * KB:(b + 1) * KB]
        x1t = x1t_sb[:, b * KB:(b + 1) * KB]
        xff = xff_sb[:, b * KB:(b + 1) * KB]
        for c in range(DC):
            E("tensor", "transpose", TR_ps[:, c * 128:(c + 1) * 128],
              zf_sb[:, tt * KB + c * 128:tt * KB + (c + 1) * 128], id_bf[:],
              w=(((ve, ve_f2b),) + fst["tr"][0] + fst["tr"][1]) if c == 0 else ())
        pe_tr = IL("tensor", pe)
        E("vector", "tensor_copy", zfT[:].rearrange("p (c n) -> p c n", n=128),
          TR_ps[:].rearrange("p (c n) -> p c n", n=128), w=((pe, pe_tr),))
        ve_zfT = IL("vector", ve)
        fst["tr"] = [((ve, ve_zfT),), ((ve, ve_zfT),)]
        for half in range(2):
            mm_group(Sb(half),
                     [zfT[:, c * 128:(c + 1) * 128] for c in range(DC)],
                     [fwot_sb[:, c * KB + half * 512:c * KB + half * 512 + 512]
                      for c in range(DC)],
                     extra_w=((ve, ve_zfT), (gq, gv_fwot)) + bank_free[half])
        pe_ao = pe.n
        xnv = p.dma("sync", xnl, xn_h[tt * 128:(tt + 1) * 128, :], sq,
                    w=xn_free[b] + ((sq, sq_p0), (pe, pe_f2b)))
        E("vector", "tensor_add", x1t[:, 0:512], xnl[:, 0:512], Sb(0),
          w=((pe, pe_ao), (sq, xnv)) + x1_free[b])
        E("vector", "tensor_add", x1t[:, 512:1024], xnl[:, 512:1024], Sb(1))
        ve_x1 = IL("vector", ve)
        xn_free[b] = ((ve, ve_x1),)
        bank_free[0] = ((ve, ve_x1),)
        bank_free[1] = ((ve, ve_x1),)
        x1v = p.dma("sync", x1_h[tt * 128:(tt + 1) * 128, :], x1t, sq, w=((ve, ve_x1),))
        if "norm2" in _F2C_SKIP:
            continue
        E("scalar", "activation", sqd_sb[:], x1t, AF.Square, accum_out=ssum_sb[:],
          w=((ve, ve_x1),))
        v = IL("scalar", se)
        E("scalar", "activation", ssum_sb[:], ssum_sb[:], AF.Ln, scale=1.0 / D,
          bias=eps6_sb[:], w=((se, v), (ve, ve_onesc)))
        v = IL("scalar", se)
        E("scalar", "activation", rl_sb[:], ssum_sb[:], AF.Exp, scale=-0.5, w=((se, v),))
        se_r2 = IL("scalar", se)
        E("vector", "scalar_tensor_tensor", xff, x1t, rl_sb[:], g2b_sb[:],
          ALU.mult, ALU.mult, w=((se, se_r2), (ve, ve_bcast)) + xff_free[b])
        ve_xf = IL("vector", ve)
        x1_free[b] = ((sq, x1v), (ve, ve_xf))
        if "xft" in _F2C_SKIP:
            continue
        for c in range(DC):
            E("tensor", "transpose", S_ps[:, 1024 + c * 128:1024 + c * 128 + 128],
              xff[:, c * 128:(c + 1) * 128], idf_sb[:],
              w=(((ve, ve_xf), (sq, sq_idf)) + bank_free[2] + bank_free[3] + b23_free)
                if c == 0 else ())
        pe_xt = IL("tensor", pe)
        xff_free[b] = ((pe, pe_xt),)
        if "xfc" in _F2C_SKIP:
            E("vector", "engine_nop", w=((pe, pe_xt),))
            ve_xfT = IL("vector", ve)
            E("scalar", "nop", w=((pe, pe_xt),))
            se_xfT = IL("scalar", se)
            b23_free = ((ve, ve_xfT), (se, se_xfT))
            bank_free[2] = ((ve, ve_xfT),)
            bank_free[3] = ((se, se_xfT),)
            continue
        E("vector", "tensor_copy", xfTf_sb[:].rearrange("p (c n) -> p c n", n=128),
          S_ps[:, 1024:2048].rearrange("p (c n) -> p c n", n=128), w=((pe, pe_xt),))
        ve_xfT = IL("vector", ve)
        E("scalar", "activation",
          xf_T[:].rearrange("p (c n) -> p c n", n=T)[:, :, tt * 128:(tt + 1) * 128],
          S_ps[:, 1024:2048].rearrange("p (c n) -> p c n", n=128), AF.Copy,
          w=((pe, pe_xt), (ve, ve_xfT)))
        se_xfT = IL("scalar", se)
        b23_free = ((ve, ve_xfT), (se, se_xfT))
        bank_free[2] = ((ve, ve_xfT),)
        bank_free[3] = ((se, se_xfT),)
        if "router" in _F2C_SKIP:
            continue
        v = mm_group(AUX_ps[:, 0:8],
                     [xfTf_sb[:, c * 128:(c + 1) * 128] for c in range(DC)],
                     [wrt_sb[:, c * 8:(c + 1) * 8] for c in range(DC)],
                     extra_w=((ve, ve_xfT), (sq, sq_wrt)) + aux_r)
        E("vector", "tensor_add", lg_sb[:], AUX_ps[:, 0:8], ebb_sb[:],
          w=((pe, v), (ve, ve_bcast)))
        vv = IL("vector", ve)
        aux_r = ((ve, vv),)
        E("vector", "tensor_reduce", m1_sb[:], lg_sb[:], AX.X, ALU.max, negate=True,
          w=((ve, vv),))
        vv = IL("vector", ve)
        E("scalar", "activation", rwu_sb[:], lg_sb[:], AF.Exp, bias=m1_sb[:],
          accum_out=den_sb[:], w=((ve, vv),))
        sv = IL("scalar", se)
        E("vector", "reciprocal", den_sb[:], den_sb[:], w=((se, sv),))
        vv = IL("vector", ve)
        E("vector", "tensor_scalar_mul", rw_sb[:], rwu_sb[:], den_sb[:], w=((ve, vv),))
        vv = IL("vector", ve)
        E("vector", "tensor_reduce", m1_sb[:], rw_sb[:], AX.X, ALU.max, w=((ve, vv),))
        vv = IL("vector", ve)
        E("vector", "tensor_single_scalar", eq1_sb[:], rw_sb[:], m1_sb[:],
          ALU.is_equal, w=((ve, vv),))
        vv = IL("vector", ve)
        E("vector", "scalar_tensor_tensor", rw2_sb[:], eq1_sb[:], -2.0, rw_sb[:],
          ALU.mult, ALU.add, w=((ve, vv),))
        vv = IL("vector", ve)
        E("vector", "tensor_reduce", m2_sb[:], rw2_sb[:], AX.X, ALU.max, w=((ve, vv),))
        vv = IL("vector", ve)
        E("vector", "tensor_single_scalar", eq2_sb[:], rw2_sb[:], m2_sb[:],
          ALU.is_equal, w=((ve, vv),))
        E("vector", "tensor_copy", e12_sb[:, 0:1], m1_sb[:])
        E("vector", "tensor_copy", e12_sb[:, 1:2], m2_sb[:])
        vv = IL("vector", ve)
        E("scalar", "activation", e12_sb[:], e12_sb[:], AF.Exp, w=((ve, vv),))
        sv = IL("scalar", se)
        E("vector", "reduce_sum", den_sb[:], e12_sb[:], AX.X, w=((se, sv),))
        vv = IL("vector", ve)
        E("vector", "reciprocal", den_sb[:], den_sb[:], w=((ve, vv),))
        vv = IL("vector", ve)
        E("vector", "tensor_scalar_mul", w12_sb[:], e12_sb[:], den_sb[:], w=((ve, vv),))
        vv = IL("vector", ve)
        E("vector", "tensor_scalar_mul", eq1_sb[:], eq1_sb[:], w12_sb[:, 0:1],
          w=((ve, vv),))
        vv = IL("vector", ve)
        E("vector", "scalar_tensor_tensor", cmb_sb[:, tt * 8:(tt + 1) * 8],
          eq2_sb[:], w12_sb[:, 1:2], eq1_sb[:], ALU.mult, ALU.add, w=((ve, vv),))
        IL("vector", ve)
    sq_f2c = sq.n
    se_f2c = se.n

    if stage == "x1":
        __fence = ((pe, pe.n), (ve, ve.n), (se, se.n))
        _copy_out(p, sq, x1_h, out_d, xls_sb, sq_f2c, extra=__fence)
        return _close(nc, p)

    NSLAB = 4
    SL = T // NSLAB
    pe_f2c = pe.n
    fw3 = fence()
    ring_free = [fw3, fw3, fw3]
    hb_free = {i: bank_free[i] for i in range(4)}
    eob_free = {0: fst["aux"], 1: ()}
    tmps_free = ()
    moe_free = ((se, se_f2c),)
    prev_ve_h = 0
    prev_pe_eo = 0
    for s in range(NSLAB):
        ve_moez = 0
        for e in range(9):
            mi = (3 + e * 3) if e < 8 else 0
            wvs = []
            for k in range(3):
                wvs.append(p.dma("gpsimd", wring[:, k * 8 * KB:(k + 1) * 8 * KB],
                                 ewbf_h[mi + k], gq, w=ring_free[k]))
            w1b = wring[:, 0:8 * KB]
            w3b = wring[:, 8 * KB:16 * KB]
            w2b = wring[:, 16 * KB:24 * KB]
            for fc in range(DC):
                for th in range(2):
                    bk = (fc * 2 + th) % 2
                    v = mm_group(
                        Sb(bk),
                        [w3b[:, c * KB + fc * 128:c * KB + fc * 128 + 128]
                         for c in range(DC)],
                        [xf_T[:, c * T + s * SL + th * 512:c * T + s * SL + th * 512 + 512]
                         for c in range(DC)],
                        extra_w=hb_free[bk] + ((gq, wvs[1]), (se, se_f2c),
                                               (ve, prev_ve_h)))
                    E("scalar", "activation",
                      h3_sb[:, fc * KB + th * 512:fc * KB + th * 512 + 512], Sb(bk),
                      AF.Copy, w=((pe, v),))
                    hb_free[bk] = ((se, IL("scalar", se)),)
            se_h3 = se.n
            for fc in range(DC):
                for th in range(2):
                    bk = 2 + (fc * 2 + th) % 2
                    v = mm_group(
                        Sb(bk),
                        [w1b[:, c * KB + fc * 128:c * KB + fc * 128 + 128]
                         for c in range(DC)],
                        [xf_T[:, c * T + s * SL + th * 512:c * T + s * SL + th * 512 + 512]
                         for c in range(DC)],
                        extra_w=hb_free[bk] + ((gq, wvs[0]),))
                    E("scalar", "activation", tmps_sb[:], Sb(bk), AF.Silu,
                      w=((pe, v),) + tmps_free)
                    sv = IL("scalar", se)
                    E("vector", "tensor_mul",
                      h_sb[:, fc * KB + th * 512:fc * KB + th * 512 + 512],
                      tmps_sb[:], h3_sb[:, fc * KB + th * 512:fc * KB + th * 512 + 512],
                      w=((se, sv), (se, se_h3), (pe, prev_pe_eo)))
                    vv = IL("vector", ve)
                    hb_free[bk] = ((ve, vv),)
                    tmps_free = ((ve, vv),)
            ve_h = ve.n
            pe_h1 = pe.n
            for tl in range(SL // 128):
                for dhh in range(2):
                    eops = AUX_ps if dhh == 0 else SPR_ps
                    v = mm_group(
                        eops[:, 0:512],
                        [h_sb[:, fc * KB + tl * 128:fc * KB + tl * 128 + 128]
                         for fc in range(DC)],
                        [w2b[:, fc * KB + dhh * 512:fc * KB + dhh * 512 + 512]
                         for fc in range(DC)],
                        extra_w=((ve, ve_h), (gq, wvs[2])) + eob_free[dhh])
                    dst = moe_sb[:, tl * KB + dhh * 512:tl * KB + dhh * 512 + 512]
                    scl = cmb_sb[:, (s * 8 + tl) * 8 + e:(s * 8 + tl) * 8 + e + 1] \
                        if e < 8 else 1.0
                    if e == 0:
                        E("vector", "tensor_scalar_mul", dst, eops[:, 0:512], scl,
                          w=((pe, v),) + moe_free)
                    else:
                        E("vector", "scalar_tensor_tensor", dst, eops[:, 0:512], scl,
                          dst, ALU.mult, ALU.add, w=((pe, v),))
                    eob_free[dhh] = ((ve, IL("vector", ve)),)
            ring_free[0] = ((pe, pe_h1),)
            ring_free[1] = ((pe, pe_h1),)
            ring_free[2] = ((pe, pe.n),)
            prev_ve_h = ve_h
            prev_pe_eo = pe.n
        ve_moe = ve.n
        xl_free = sl_free = x1l_free = ()
        prev_store = ()
        for tl in range(SL // 128):
            tt = s * (SL // 128) + tl
            xv = p.dma("sync", xls_sb[:], x_d[tt * 128:(tt + 1) * 128, :], sq,
                       w=xl_free + ((pe, pe_f2c),))
            sv = p.dma("sync", sls_sb[:], s_h[tt * 128:(tt + 1) * 128, :], sq,
                       w=sl_free + ((sq, sq_p1c),))
            x1v = p.dma("sync", x1l_sb[:], x1_h[tt * 128:(tt + 1) * 128, :], sq,
                        w=x1l_free + ((sq, sq_f2c),))
            E("vector", "tensor_add", gl_sb[:], x1l_sb[:],
              moe_sb[:, tl * KB:(tl + 1) * KB], w=((sq, x1v), (ve, ve_moe)))
            x1l_free = ((ve, IL("vector", ve)),)
            E("scalar", "activation", gl_sb[:], gl_sb[:], AF.Silu, w=((ve, ve.n),))
            sv2 = IL("scalar", se)
            E("vector", "tensor_mul", gl_sb[:], gl_sb[:], sls_sb[:],
              w=((se, sv2), (sq, sv)))
            sl_free = ((ve, IL("vector", ve)),)
            E("vector", "tensor_add", outt_sb[:], gl_sb[:], xls_sb[:],
              w=((ve, ve.n), (sq, xv)) + prev_store)
            xl_free = ((ve, IL("vector", ve)),)
            ov = p.dma("sync", out_d[tt * 128:(tt + 1) * 128, :], outt_sb[:], sq,
                       w=((ve, ve.n),))
            prev_store = ((sq, ov),)
        moe_free = ((ve, ve.n),)

    return _close(nc, p)


def _copy_out(p, sq, src_h, out_d, tile_sb, sq_src, extra=()):
    last = ((sq, sq_src),) + extra
    for tt in range(NT):
        v = p.dma("sync", tile_sb[:], src_h[tt * 128:(tt + 1) * 128, :], sq, w=last)
        ov = p.dma("sync", out_d[tt * 128:(tt + 1) * 128, :], tile_sb[:], sq,
                   w=((sq, v),))
        last = ((sq, ov),)


def _close(nc, p):
    sems = {s.name: s for s in p.sems}
    p.emit("sync", "nop", w=((sems["sq"], sems["sq"].n), (sems["gq"], sems["gq"].n)))
    p.emit("gpsimd", "nop", w=((sems["gq"], sems["gq"].n),))
    with contextlib.ExitStack() as stack:
        handles = {}
        for s in p.sems:
            handles[s] = stack.enter_context(nc.semaphore(s.name))
        block = stack.enter_context(nc.Block())
        for eng in Prog.ENGINES:
            def make(eng_name):
                def f(eng_obj):
                    for meth, args, kw, waits, incs in p.ops[eng_name]:
                        for sm, v in waits:
                            eng_obj.wait_ge(handles[sm], v)
                        ins = getattr(eng_obj, meth)(*args, **kw)
                        for sm, k in incs:
                            ins = ins.then_inc(handles[sm], k)
                return f
            getattr(block, eng)(make(eng))
    return nc


def _prep_inputs(inp):
    f32 = lambda a: np.ascontiguousarray(np.asarray(a, dtype=np.float32))
    tr = lambda a: np.ascontiguousarray(np.asarray(a, dtype=np.float32).T)
    ews = np.empty((27, D, D), np.float32)
    ews[0] = np.asarray(inp["L0_ffn_sW1"], np.float32).T
    ews[1] = np.asarray(inp["L0_ffn_sW3"], np.float32).T
    ews[2] = np.asarray(inp["L0_ffn_sW2"], np.float32).T
    eW1 = np.asarray(inp["L0_ffn_eW1"], np.float32)
    eW2 = np.asarray(inp["L0_ffn_eW2"], np.float32)
    eW3 = np.asarray(inp["L0_ffn_eW3"], np.float32)
    for e in range(8):
        ews[3 + 3 * e + 0] = eW1[e].T
        ews[3 + 3 * e + 1] = eW3[e].T
        ews[3 + 3 * e + 2] = eW2[e].T
    mask = np.where(np.arange(128)[None, :] > np.arange(128)[:, None],
                    np.float32(-1e10), np.float32(0.0)).astype(np.float32)
    common = dict(
        wkt=tr(np.asarray(inp["L0_attn_W_K"], np.float32).reshape(D, D)),
        wqt=tr(np.asarray(inp["L0_attn_W_Q"], np.float32).reshape(D, D)),
        wvt=tr(np.asarray(inp["L0_attn_W_V"], np.float32).reshape(D, D)),
        wot=tr(inp["L0_attn_W_O"]),
        fwqt=tr(inp["L0_ffn_Wq"]), fwkct=tr(inp["L0_ffn_Wkc"]),
        fwvct=tr(inp["L0_ffn_Wvc"]), fwot=tr(inp["L0_ffn_Wo"]),
        fwrt=tr(inp["L0_ffn_Wr"]), ews=np.ascontiguousarray(ews),
        g1=f32(inp["L0_ffn_g1"]), g2=f32(inp["L0_ffn_g2"]),
        eb=f32(inp["L0_ffn_expert_bias"]),
        ident=np.eye(128, dtype=np.float32), mask=np.ascontiguousarray(mask),
        ones=np.ones((1, 128), np.float32),
    )
    x = np.asarray(inp["L0_attn_x"], np.float32)
    return [dict(common, x=np.ascontiguousarray(x[b])) for b in range(8)]


def kernel(**inputs):
    in_maps = _prep_inputs(inputs)
    nc = build(_STAGE)
    res = run_bass_kernel_spmd(nc, in_maps, core_ids=list(range(8)))
    return np.stack([np.asarray(r["out"]) for r in res.results], axis=0)


# version 17 (final)
# speedup vs baseline: 1.0165x; 1.0165x over previous
import contextlib
import numpy as np
import concourse.bass as bass
import concourse.mybir as mybir
from concourse.bass_utils import run_bass_kernel_spmd

F32 = mybir.dt.float32
BF16 = mybir.dt.bfloat16
AF = mybir.ActivationFunctionType
ALU = mybir.AluOpType
AX = mybir.AxisListType

T = 4096
D = 1024
NT = T // 128
DC = D // 128
KB = 1024

_STAGE = "full"
_DBG_SKIP = ()
_NG, _NH, _NQ, _Q0 = 4, 4, 32, 0
_FL_SKIP = ()
_F2C_SKIP = ()


class Sem:
    __slots__ = ("name", "n")

    def __init__(self, name):
        self.name = name
        self.n = 0


class Prog:
    ENGINES = ("tensor", "vector", "scalar", "sync", "gpsimd")

    def __init__(self):
        self.ops = {e: [] for e in self.ENGINES}
        self.sems = []
        self._waited = {e: {} for e in self.ENGINES}

    def sem(self, name):
        s = Sem(name)
        self.sems.append(s)
        return s

    def emit(self, eng, meth, *args, w=(), **kw):
        ww = []
        for s, v in w:
            if v > 0 and self._waited[eng].get(s, 0) < v:
                self._waited[eng][s] = v
                ww.append((s, v))
        self.ops[eng].append([meth, args, kw, tuple(ww), ()])

    def inc_last(self, eng, sem, k=1):
        sem.n += k
        op = self.ops[eng][-1]
        op[4] = op[4] + ((sem, k),)
        return sem.n

    def dma(self, eng, out, in_, sem, w=(), **kw):
        self.emit(eng, "dma_start", out, in_, w=w, **kw)
        return self.inc_last(eng, sem, 16)


def build(stage="full"):
    skip = _DBG_SKIP
    nc = bass.Bass("TRN2", target_bir_lowering=False, debug=False)

    x_d = nc.dram_tensor("x", [T, D], F32, kind="ExternalInput").ap()
    wkt_d = nc.dram_tensor("wkt", [D, D], F32, kind="ExternalInput").ap()
    wqt_d = nc.dram_tensor("wqt", [D, D], F32, kind="ExternalInput").ap()
    wvt_d = nc.dram_tensor("wvt", [D, D], F32, kind="ExternalInput").ap()
    wot_d = nc.dram_tensor("wot", [D, D], F32, kind="ExternalInput").ap()
    fwqt_d = nc.dram_tensor("fwqt", [D, D], F32, kind="ExternalInput").ap()
    fwkct_d = nc.dram_tensor("fwkct", [D, 256], F32, kind="ExternalInput").ap()
    fwvct_d = nc.dram_tensor("fwvct", [D, 256], F32, kind="ExternalInput").ap()
    fwot_d = nc.dram_tensor("fwot", [D, D], F32, kind="ExternalInput").ap()
    fwrt_d = nc.dram_tensor("fwrt", [D, 8], F32, kind="ExternalInput").ap()
    ews_d = nc.dram_tensor("ews", [27, D, D], F32, kind="ExternalInput").ap()
    g1_d = nc.dram_tensor("g1", [D], F32, kind="ExternalInput").ap()
    g2_d = nc.dram_tensor("g2", [D], F32, kind="ExternalInput").ap()
    eb_d = nc.dram_tensor("eb", [8], F32, kind="ExternalInput").ap()
    ident_d = nc.dram_tensor("ident", [128, 128], F32, kind="ExternalInput").ap()
    mask_d = nc.dram_tensor("mask", [128, 128], F32, kind="ExternalInput").ap()
    ones_d = nc.dram_tensor("ones", [1, 128], F32, kind="ExternalInput").ap()
    out_d = nc.dram_tensor("out", [T, D], F32, kind="ExternalOutput").ap()
    dbg_d = nc.dram_tensor("dbg", [128, NT * 4], F32, kind="ExternalOutput").ap() \
        if stage == "xn" else None

    xn_h = nc.dram_tensor("xn_h", [T, D], F32, kind="Internal").ap()
    s_h = nc.dram_tensor("s_h", [T, D], F32, kind="Internal").ap()
    x1_h = nc.dram_tensor("x1_h", [T, D], F32, kind="Internal").ap()
    z_h = nc.dram_tensor("z_h", [T, D], BF16, kind="Internal").ap()
    ewbf_h = nc.dram_tensor("ewbf", [27, 128, 8 * KB], BF16, kind="Internal").ap()

    def sb(name, shape, dtype, off):
        return nc.alloc_sbuf_tensor_at(name, shape, dtype, offset=off + 16 * KB + 128)

    xn_T = sb("xn_T", [128, 8 * T], BF16, 0)
    qt_f = sb("qt_f", [128, 8 * T], BF16, 0)
    xf_T = sb("xf_T", [128, 8 * T], BF16, 0)
    nx_T = sb("nx_T", [128, 8 * T], BF16, 64 * KB)
    zf_sb = sb("zf_sb", [128, NT * KB], BF16, 64 * KB)
    kt_g = sb("kt_g", [128, 2 * T], BF16, 64 * KB)
    qt_g = sb("qt_g", [128, 2 * T], BF16, 80 * KB)
    v_g = sb("v_g", [128, NT * 256], BF16, 96 * KB)
    zg_sb = sb("zg_sb", [128, NT * 256], BF16, 112 * KB)
    h3_sb = sb("h3_sb", [128, 8 * KB], BF16, 64 * KB)
    h_sb = sb("h_sb", [128, 8 * KB], BF16, 80 * KB)
    moe_sb = sb("moe_sb", [128, 8 * KB], F32, 96 * KB)

    xt_sb = sb("xt_sb", [128, 2 * KB], F32, 128 * KB)
    xnf_sb = sb("xnf_sb", [128, 2 * KB], F32, 136 * KB)
    xnb_sb = sb("xnb_sb", [128, 2 * KB], BF16, 144 * KB)

    wk_sb = sb("wk_sb", [128, 2 * KB], BF16, 128 * KB)
    wq_sb = sb("wq_sb", [128, 2 * KB], BF16, 132 * KB)
    wv_sb = sb("wv_sb", [128, 2 * KB], BF16, 136 * KB)
    p_sb = sb("p_sb", [128, 3 * KB], BF16, 140 * KB)
    pt_sb = sb("pt_sb", [128, 3 * KB], BF16, 146 * KB)
    stg_sb = sb("stg_sb", [128, 8 * KB], BF16, 152 * KB)

    wot_sb = sb("wot_sb", [128, 8 * KB], BF16, 168 * KB)
    zt_sb = sb("zt_sb", [128, 2 * KB], BF16, 184 * KB)
    zT_sb = sb("zT_sb", [128, 2 * KB], BF16, 188 * KB)
    sst_sb = sb("sst_sb", [128, 2 * KB], F32, 128 * KB)

    sqf_sb = sb("sqf_sb", [128, 2 * 512], F32, 136 * KB)
    rrow_sb = sb("rrow_sb", [1, 2 * 512], F32, 140 * KB)
    g2r_sb = sb("g2r_sb", [1, KB], F32, 160 * KB)
    ebr_sb = sb("ebr_sb", [1, 8], F32, 164 * KB)

    fwq_sb = sb("fwq_sb", [128, 8 * KB], BF16, 128 * KB)
    fwkc_sb = sb("fwkc_sb", [128, 2 * KB], BF16, 144 * KB)
    fwvc_sb = sb("fwvc_sb", [128, 2 * KB], BF16, 148 * KB)
    kct_sb = sb("kct_sb", [128, 2 * T], BF16, 152 * KB)
    vc_sb = sb("vc_sb", [128, NT * 256], BF16, 168 * KB)

    fwot_sb = sb("fwot_sb", [128, 8 * KB], BF16, 128 * KB)
    zft_sb = sb("zft_sb", [128, 2 * KB], BF16, 144 * KB)
    zfT_sb = sb("zfT_sb", [128, 2 * KB], BF16, 148 * KB)
    xnl_sb = sb("xnl_sb", [128, 2 * KB], F32, 152 * KB)
    x1t_sb = sb("x1t_sb", [128, 2 * KB], F32, 160 * KB)
    xff_sb = sb("xff_sb", [128, 2 * KB], F32, 168 * KB)
    xfTf_sb = sb("xfTf_sb", [128, KB], F32, 176 * KB)

    wring = sb("wring", [128, 3 * 8 * KB], BF16, 128 * KB)
    xls_sb = sb("xls_sb", [128, KB], F32, 176 * KB)
    sls_sb = sb("sls_sb", [128, KB], F32, 180 * KB)
    x1l_sb = sb("x1l_sb", [128, KB], F32, 184 * KB)
    gl_sb = sb("gl_sb", [128, KB], F32, 188 * KB)
    tmps_sb = sb("tmps_sb", [128, 512], BF16, 192 * KB)
    outt_sb = sb("outt_sb", [128, KB], F32, 193 * KB)

    sqd_sb = sb("sqd_sb", [128, KB], BF16, 197 * KB)
    id_bf = sb("id_bf", [128, 128], BF16, 199 * KB)
    idf_sb = sb("idf_sb", [128, 128], F32, 199 * KB + 256)
    mask_sb = sb("mask_sb", [128, 128], F32, 199 * KB + 768)
    g2b_sb = sb("g2b_sb", [128, KB], F32, 200 * KB + 256)
    cmb_sb = sb("cmb_sb", [128, NT * 8], F32, 204 * KB + 256)
    onesr_sb = sb("onesr_sb", [1, 128], F32, 205 * KB + 256)
    onesc_sb = sb("onesc_sb", [128, 1], F32, 205 * KB + 768)
    g1_sb = sb("g1_sb", [128, 8], F32, 205 * KB + 800)
    wrt_sb = sb("wrt_sb", [128, 64], F32, 205 * KB + 832)
    ebb_sb = sb("ebb_sb", [128, 8], F32, 205 * KB + 1088)
    stoff = 206 * KB + 256
    nm_sb = sb("nm_sb", [128, 1], F32, stoff + 0)
    nm2_sb = sb("nm2_sb", [128, 1], F32, stoff + 32)
    pm_sb = sb("pm_sb", [128, 1], F32, stoff + 64)
    a0_sb = sb("a0_sb", [128, 1], F32, stoff + 96)
    l0_sb = sb("l0_sb", [128, 1], F32, stoff + 128)
    l1_sb = sb("l1_sb", [128, 1], F32, stoff + 160)
    rl_sb = sb("rl_sb", [128, 1], F32, stoff + 192)
    ssum_sb = sb("ssum_sb", [128, 1], F32, stoff + 224)
    osb_sb = sb("osb_sb", [128, 256], F32, stoff + 256)
    lg_sb = sb("lg_sb", [128, 8], F32, stoff + 1280)
    rwu_sb = sb("rwu_sb", [128, 8], F32, stoff + 1312)
    rw_sb = sb("rw_sb", [128, 8], F32, stoff + 1344)
    eq1_sb = sb("eq1_sb", [128, 8], F32, stoff + 1376)
    eq2_sb = sb("eq2_sb", [128, 8], F32, stoff + 1408)
    rw2_sb = sb("rw2_sb", [128, 8], F32, stoff + 1440)
    e12_sb = sb("e12_sb", [128, 2], F32, stoff + 1472)
    w12_sb = sb("w12_sb", [128, 2], F32, stoff + 1504)
    den_sb = sb("den_sb", [128, 1], F32, stoff + 1536)
    m1_sb = sb("m1_sb", [128, 1], F32, stoff + 1568)
    m2_sb = sb("m2_sb", [128, 1], F32, stoff + 1600)
    l2_sb = sb("l2_sb", [128, 1], F32, stoff + 1632)
    eps5_sb = sb("eps5_sb", [128, 1], F32, 205 * KB + 1152)
    eps6_sb = sb("eps6_sb", [128, 1], F32, 205 * KB + 1184)
    dbg_sb = sb("dbg_sb", [128, NT * 4], F32, 188 * KB) if stage == "xn" else None

    S_ps = nc.alloc_psum_tensor("S_ps", [128, 4 * 512], F32)
    TRA_ps = nc.alloc_psum_tensor("TRA_ps", [128, 1024], BF16)
    TRB_ps = nc.alloc_psum_tensor("TRB_ps", [128, 1024], BF16)
    AUX_ps = nc.alloc_psum_tensor("AUX_ps", [128, 512], F32)
    SPR_ps = nc.alloc_psum_tensor("SPR_ps", [128, 512], F32)
    TR_ps = TRA_ps

    def Sb(i):
        return S_ps[:, i * 512:(i + 1) * 512]

    p = Prog()
    pe, ve, se, sq, gq = (p.sem(n) for n in ("pe", "ve", "se", "sq", "gq"))
    E = p.emit
    IL = p.inc_last

    gq_id = p.dma("gpsimd", id_bf[:], ident_d, gq)
    sq_mask = p.dma("sync", mask_sb[:], mask_d, sq)
    sq_idf = p.dma("sync", idf_sb[:], ident_d, sq)
    sq_ones = p.dma("sync", onesr_sb[:], ones_d, sq)
    sq_g1 = p.dma("sync", g1_sb[:], g1_d.rearrange("(c p) -> p c", p=128), sq,
                  allow_slow_non_contiguous=True)
    sq_wrt = p.dma("sync", wrt_sb[:].rearrange("p (c n) -> p c n", c=8),
                   fwrt_d.rearrange("(c p) n -> p c n", p=128), sq)
    sq_g2r = p.dma("sync", g2r_sb[:], g2_d.rearrange("(a n) -> a n", a=1), sq)
    sq_ebr = p.dma("sync", ebr_sb[:], eb_d.rearrange("(a n) -> a n", a=1), sq)
    E("vector", "memset", onesc_sb[:], 1.0)
    E("vector", "memset", eps5_sb[:], 1e-5)
    E("vector", "memset", eps6_sb[:], 1e-6)
    ve_onesc = IL("vector", ve)

    skip = _DBG_SKIP
    if "bcast" not in skip:
        E("tensor", "matmul", Sb(0), onesr_sb[:], g2r_sb[:, 0:512], start=True, stop=True,
          w=((sq, sq_g2r), (sq, sq_ones)))
    if "bcast" not in skip:
        E("tensor", "matmul", Sb(1), onesr_sb[:], g2r_sb[:, 512:1024], start=True, stop=True)
        pe_g2b = IL("tensor", pe)
        E("tensor", "matmul", AUX_ps[:, 0:8], onesr_sb[:], ebr_sb[:], start=True, stop=True,
          w=((sq, sq_ebr),))
        pe_ebb = IL("tensor", pe)
        E("vector", "tensor_copy", g2b_sb[:, 0:512], Sb(0), w=((pe, pe_g2b),))
        E("vector", "tensor_copy", g2b_sb[:, 512:1024], Sb(1))
        E("vector", "tensor_copy", ebb_sb[:], AUX_ps[:, 0:8], w=((pe, pe_ebb),))
    else:
        E("vector", "engine_nop")
    ve_bcast = IL("vector", ve)

    fst = {
        "s_free": ((ve, ve_bcast),),
        "aux": ((ve, ve_bcast),),
        "tr": [((ve, 0),), ((ve, 0),)],
        "p_pe": 0,
        "pt_pe": 0,
    }
    bank_free = {i: ((ve, ve_bcast),) for i in range(4)}

    xt_free = [(), ()]
    xnf_free = [(), ()]
    xnb_free = [(), ()]
    se_sq_prev = 0
    for tt in range(NT):
        b = tt % 2
        xt = xt_sb[:, b * KB:(b + 1) * KB]
        xnf = xnf_sb[:, b * KB:(b + 1) * KB]
        xnb = xnb_sb[:, b * KB:(b + 1) * KB]
        xv = p.dma("sync", xt, x_d[tt * 128:(tt + 1) * 128, :], sq, w=xt_free[b])
        E("vector", "tensor_reduce", nm_sb[:], xt, AX.X, ALU.add, negate=True,
          w=((sq, xv), (se, se_sq_prev)))
        v = IL("vector", ve)
        E("vector", "tensor_scalar_mul", nm_sb[:], nm_sb[:], 1.0 / D, w=((ve, v),))
        ve_nm = IL("vector", ve)
        E("scalar", "activation", sqd_sb[:], xt, AF.Square, bias=nm_sb[:],
          accum_out=ssum_sb[:], w=((ve, ve_nm),))
        v = IL("scalar", se)
        se_sq_prev = v
        E("scalar", "activation", ssum_sb[:], ssum_sb[:], AF.Ln, scale=1.0 / D,
          bias=eps5_sb[:], w=((se, v), (ve, ve_onesc)))
        v = IL("scalar", se)
        E("scalar", "activation", rl_sb[:], ssum_sb[:], AF.Exp, scale=-0.5, w=((se, v),))
        se_st = IL("scalar", se)
        E("vector", "tensor_scalar", xnf, xt, nm_sb[:], rl_sb[:], ALU.add, ALU.mult,
          w=((se, se_st),) + xnf_free[b])
        ve_xnf = IL("vector", ve)
        xt_free[b] = ((ve, ve_xnf),)
        sv = p.dma("sync", xn_h[tt * 128:(tt + 1) * 128, :], xnf, sq, w=((ve, ve_xnf),))
        if "cast" not in skip:
            E("vector", "tensor_copy", xnb, xnf, w=((ve, ve_xnf),) + xnb_free[b])
            ve_cast = IL("vector", ve)
            xnf_free[b] = ((ve, ve_cast), (sq, sv))
        else:
            ve_cast = 0
            xnf_free[b] = ((sq, sv),)
        if "tr" not in skip:
            for c in range(DC):
                E("tensor", "transpose", TR_ps[:, c * 128:(c + 1) * 128],
                  xnb[:, c * 128:(c + 1) * 128], id_bf[:],
                  w=((ve, ve_cast), (gq, gq_id)) + fst["tr"][0] + fst["tr"][1]
                    if c == 0 else ())
            pe_tr = IL("tensor", pe)
            xnb_free[b] = ((pe, pe_tr),)
            for c in range(DC):
                E("vector", "tensor_copy",
                  xn_T[:, c * T + tt * 128:c * T + tt * 128 + 128],
                  TR_ps[:, c * 128:(c + 1) * 128],
                  w=((pe, pe_tr),) if c == 0 else ())
            v = IL("vector", ve)
            fst["tr"] = [((ve, v),), ((ve, v),)]
        if stage == "xn":
            E("vector", "tensor_copy", dbg_sb[:, tt * 4:tt * 4 + 1], nm_sb[:],
              w=((ve, ve_xnf),))
            E("vector", "tensor_copy", dbg_sb[:, tt * 4 + 1:tt * 4 + 2], ssum_sb[:])
            E("vector", "tensor_copy", dbg_sb[:, tt * 4 + 2:tt * 4 + 3], rl_sb[:])
            E("vector", "tensor_copy", dbg_sb[:, tt * 4 + 3:tt * 4 + 4], eps5_sb[:])
            IL("vector", ve)
    sq_p0 = sq.n

    if stage == "xn":
        p.dma("sync", dbg_d, dbg_sb[:], sq, w=((ve, ve.n),))
        _copy_out(p, sq, xn_h, out_d, xls_sb, sq_p0, extra=__fence)
        return _close(nc, p)

    sq_conv = [0]
    def emit_conversions():
        stg_free = [(), ()]
        for m in range(27):
            for hh in range(2):
                b = (m * 2 + hh) % 2
                stg = stg_sb[:, b * 4 * KB:(b + 1) * 4 * KB]
                src = ews_d[m].rearrange("(c p) n -> p c n", p=128)[:, hh * 4:(hh + 1) * 4, :]
                gv = p.dma("gpsimd", stg[:].rearrange("p (c n) -> p c n", c=4), src, gq,
                           w=stg_free[b])
                sv = p.dma("sync", ewbf_h[m][:, hh * 4 * KB:(hh + 1) * 4 * KB], stg, sq,
                           w=((gq, gv),))
                stg_free[b] = ((sq, sv),)

    def fence():
        return ((pe, pe.n), (ve, ve.n), (se, se.n), (sq, sq.n))

    def mm_group(dst_ps, lhsT_list, rhs_list, extra_w=()):
        n = len(lhsT_list)
        for i, (lh, rh) in enumerate(zip(lhsT_list, rhs_list)):
            E("tensor", "matmul", dst_ps, lh, rh, start=(i == 0), stop=(i == n - 1),
              w=extra_w if i == 0 else ())
        return IL("tensor", pe)

    def proj_fm(out_sb, out_cols, w_sb, w_cols, act_T, extra_w=()):
        first = True
        for oc in range(out_cols):
            for ts in range(T // 512):
                bkk = (oc * (T // 512) + ts) % 4
                v = mm_group(
                    Sb(bkk),
                    [w_sb[:, c * w_cols + oc * 128:c * w_cols + oc * 128 + 128]
                     for c in range(DC)],
                    [act_T[:, c * T + ts * 512:c * T + ts * 512 + 512] for c in range(DC)],
                    extra_w=bank_free[bkk] + (extra_w if first else ()))
                first = False
                E("vector", "tensor_copy",
                  out_sb[:, oc * T + ts * 512:oc * T + ts * 512 + 512], Sb(bkk),
                  w=((pe, v),))
                bank_free[bkk] = ((ve, IL("vector", ve)),)

    def proj_tm_256(out_sb, w_sb, act_T, extra_w=()):
        first = True
        for tt in range(NT):
            bkk = tt % 4
            v = mm_group(
                S_ps[:, bkk * 512:bkk * 512 + 256],
                [act_T[:, c * T + tt * 128:c * T + tt * 128 + 128] for c in range(DC)],
                [w_sb[:, c * 256:(c + 1) * 256] for c in range(DC)],
                extra_w=bank_free[bkk] + (extra_w if first else ()))
            first = False
            E("vector", "tensor_copy", out_sb[:, tt * 256:(tt + 1) * 256],
              S_ps[:, bkk * 512:bkk * 512 + 256], w=((pe, v),))
            bank_free[bkk] = ((ve, IL("vector", ve)),)

    def flash(L, scale, causal, qls, k_row, av_rhs, dh, z_dst, z_w):
        if causal:
            two = L > 2048
            passes = [(0, min(L, 2048))] + ([(2048, L - 2048)] if two else [])
        else:
            two = True
            passes = [(0, 2048), (2048, 2048)]
        se_a0 = 0
        for pi, (off, Lp) in enumerate(passes):
            split = Lp > 1024
            qk_inc = []
            for hb in range(2 if split else 1):
                h0 = hb * 1024
                hl_len = min(1024, Lp - h0)
                for j in range(h0 // 512, (h0 + hl_len + 511) // 512):
                    n = min(512, Lp - j * 512)
                    krs = k_row(off + j * 512, n)
                    for ci, (ql, kr) in enumerate(zip(qls, krs)):
                        E("tensor", "matmul", S_ps[:, j * 512:j * 512 + n], ql, kr,
                          start=(ci == 0), stop=(ci == len(qls) - 1),
                          w=fst["s_free"] if (j == 0 and ci == 0) else ())
                qk_inc.append(IL("tensor", pe))
            pe_qk = qk_inc[-1]
            ve_mask = 0
            if causal and off + Lp == L:
                E("vector", "tensor_tensor", S_ps[:, Lp - 128:Lp], S_ps[:, Lp - 128:Lp],
                  mask_sb[:], ALU.add, w=((pe, pe_qk), (sq, sq_mask)))
                ve_mask = IL("vector", ve)
            if split:
                la_len = 1024
                subA = bass.AP(S_ps, 0, [[4 * 512, 128], [32, la_len // 32]])
                E("vector", "tensor_reduce", m1_sb[:], subA, AX.X, ALU.max, negate=True,
                  w=((pe, qk_inc[0]),))
                IL("vector", ve)
                subB = bass.AP(S_ps, 1024, [[4 * 512, 128], [32, (Lp - 1024) // 32]])
                E("vector", "tensor_reduce", nm_sb[:], subB, AX.X, ALU.max, negate=True,
                  w=((pe, pe_qk),) + (((ve, ve_mask),) if ve_mask else ()))
                vv = IL("vector", ve)
                E("vector", "tensor_scalar", nm_sb[:], nm_sb[:], m1_sb[:], scale,
                  ALU.min, ALU.mult, w=((ve, vv),))
                vv = IL("vector", ve)
            else:
                sub = bass.AP(S_ps, 0, [[4 * 512, 128], [32, Lp // 32]])
                E("vector", "tensor_reduce", nm_sb[:], sub, AX.X, ALU.max, negate=True,
                  w=((pe, pe_qk),) + (((ve, ve_mask),) if ve_mask else ()))
                vv = IL("vector", ve)
                E("vector", "tensor_scalar_mul", nm_sb[:], nm_sb[:], scale, w=((ve, vv),))
                vv = IL("vector", ve)
            if pi == 0:
                nm_use = nm_sb
                if two:
                    E("vector", "tensor_copy", nm2_sb[:], nm_sb[:], w=((ve, vv),))
                    E("vector", "tensor_scalar_mul", pm_sb[:], nm_sb[:], -1.0)
                    vv = IL("vector", ve)
            else:
                E("vector", "tensor_tensor", nm2_sb[:], nm2_sb[:], nm_sb[:], ALU.min,
                  w=((ve, vv),))
                vv = IL("vector", ve)
                E("scalar", "activation", a0_sb[:], nm2_sb[:], AF.Exp, bias=pm_sb[:],
                  w=((ve, vv),))
                se_a0 = IL("scalar", se)
                nm_use = nm2_sb
            lpa = l0_sb if pi == 0 else l1_sb
            nla = min(1024, Lp)
            E("scalar", "activation", p_sb[:, 0:nla], S_ps[:, 0:nla], AF.Exp,
              bias=nm_use[:], scale=scale, accum_out=lpa[:],
              w=((ve, vv), (pe, fst["p_pe"])))
            se_expA = IL("scalar", se)
            if split:
                E("scalar", "activation", p_sb[:, 1024:Lp], S_ps[:, 1024:Lp], AF.Exp,
                  bias=nm_use[:], scale=scale, accum_out=l2_sb[:])
                se_expB = IL("scalar", se)
                E("vector", "tensor_add", lpa[:], lpa[:], l2_sb[:], w=((se, se_expB),))
                IL("vector", ve)
            else:
                se_expB = se_expA
            fst["s_free"] = ((se, se_expB),)
            nch = Lp // 128
            cp_inc = []
            for grp in range(0, nch, 8):
                gend = min(grp + 8, nch)
                half = (grp // 8) % 2
                trb = TRA_ps if half == 0 else TRB_ps
                se_w = se_expA if grp == 0 else se_expB
                for c in range(grp, gend):
                    E("tensor", "transpose", trb[:, (c - grp) * 128:(c - grp + 1) * 128],
                      p_sb[:, c * 128:(c + 1) * 128], id_bf[:],
                      w=(((se, se_w),) + fst["tr"][half]) if c == grp else ())
                pe_tr = IL("tensor", pe)
                E("vector", "tensor_copy", pt_sb[:, grp * 128:gend * 128],
                  trb[:, 0:(gend - grp) * 128],
                  w=((pe, pe_tr),) + (((pe, fst["pt_pe"]),) if grp == 0 else ()))
                cp_inc.append(IL("vector", ve))
                fst["tr"][half] = ((ve, cp_inc[-1]),)
            fst["p_pe"] = pe.n
            kt0 = off // 128
            for c in range(nch):
                wv = ()
                if c == 0:
                    wv = ((ve, cp_inc[0]),) + fst["aux"]
                elif c == 8 and len(cp_inc) > 1:
                    wv = ((ve, cp_inc[1]),)
                E("tensor", "matmul", AUX_ps[:, 0:dh], pt_sb[:, c * 128:(c + 1) * 128],
                  av_rhs(kt0 + c), start=(c == 0), stop=(c == nch - 1), w=wv)
            pe_av = IL("tensor", pe)
            fst["pt_pe"] = pe_av
            if not two:
                E("vector", "reciprocal", rl_sb[:], l0_sb[:], w=((se, se_expA), (ve, ve.n)))
                vv = IL("vector", ve)
                E("vector", "tensor_scalar_mul", z_dst, AUX_ps[:, 0:dh], rl_sb[:],
                  w=((ve, vv), (pe, pe_av)) + z_w)
                fst["aux"] = ((ve, IL("vector", ve)),)
            elif pi == 0:
                E("vector", "tensor_copy", osb_sb[:, 0:dh], AUX_ps[:, 0:dh],
                  w=((pe, pe_av),))
                fst["aux"] = ((ve, IL("vector", ve)),)
            else:
                E("vector", "scalar_tensor_tensor", l0_sb[:], l0_sb[:], a0_sb[:],
                  l1_sb[:], ALU.mult, ALU.add, w=((se, se_a0), (ve, ve.n)))
                vv = IL("vector", ve)
                E("vector", "reciprocal", rl_sb[:], l0_sb[:], w=((ve, vv),))
                vv = IL("vector", ve)
                E("vector", "scalar_tensor_tensor", osb_sb[:, 0:dh], osb_sb[:, 0:dh],
                  a0_sb[:], AUX_ps[:, 0:dh], ALU.mult, ALU.add, w=((ve, vv), (pe, pe_av)))
                fst["aux"] = ((ve, IL("vector", ve)),)
                E("vector", "tensor_scalar_mul", z_dst, osb_sb[:, 0:dh], rl_sb[:],
                  w=((ve, ve.n),) + z_w)
                IL("vector", ve)

    sq_z = 0
    for g in range(_NG):
        wws = []
        fw = fence()
        for w_s, w_dd in ((wk_sb, wkt_d), (wq_sb, wqt_d), (wv_sb, wvt_d)):
            gv = p.dma("gpsimd", w_s[:].rearrange("p (c n) -> p c n", c=DC),
                       w_dd.rearrange("(c p) n -> p c n", p=128)[:, :, g * 256:(g + 1) * 256],
                       gq, w=fw)
            wws.append((gq, gv))
        if g == 0:
            emit_conversions()
            sq_conv[0] = sq.n
        for i in range(4):
            bank_free[i] = bank_free[i] + fst["s_free"]
        proj_fm(kt_g[:], 2, wk_sb[:], 256, xn_T[:], extra_w=(wws[0],))
        proj_fm(qt_g[:], 2, wq_sb[:], 256, xn_T[:], extra_w=(wws[1],))
        proj_tm_256(v_g[:], wv_sb[:], xn_T[:], extra_w=(wws[2],))
        fst["s_free"] = bank_free[0] + bank_free[1] + bank_free[2] + bank_free[3] + \
            fst["s_free"]
        for hl in range(_NH):
            pl, pb = hl // 2, 64 * (hl % 2)
            for qt in range(_Q0, min(NT, _NQ)):
                flash(L=(qt + 1) * 128, scale=0.125, causal=True,
                      qls=[qt_g[pb:pb + 64, pl * T + qt * 128:pl * T + qt * 128 + 128]],
                      k_row=lambda off, n, pl=pl, pb=pb:
                          [kt_g[pb:pb + 64, pl * T + off:pl * T + off + n]],
                      av_rhs=lambda kt, hl=hl:
                          v_g[:, kt * 256 + hl * 64:kt * 256 + hl * 64 + 64],
                      dh=64,
                      z_dst=zg_sb[:, qt * 256 + hl * 64:qt * 256 + hl * 64 + 64],
                      z_w=((sq, sq_z),))
        dstz = z_h.rearrange("(t p) n -> p t n", p=128)[:, :, g * 256:(g + 1) * 256]
        sq_z = p.dma("sync", dstz, zg_sb[:].rearrange("p (t n) -> p t n", n=256), sq,
                     w=((ve, ve.n),))

    if stage == "z":
        p.dma("gpsimd", out_d, z_h, gq, w=((sq, sq_z),))
        p.emit("gpsimd", "nop", w=((gq, gq.n),))
        return _close(nc, p)

    gv_wot = p.dma("gpsimd", wot_sb[:].rearrange("p (c n) -> p c n", c=DC),
                   wot_d.rearrange("(c p) n -> p c n", p=128), gq, w=fence())
    for i in range(4):
        bank_free[i] = bank_free[i] + fst["s_free"]
    zt_free = [(), ()]
    sst_free = [(), ()]
    for tt in range(NT):
        b = tt % 2
        zt = zt_sb[:, b * KB:(b + 1) * KB]
        zT = zT_sb[:, b * KB:(b + 1) * KB]
        sst = sst_sb[:, b * KB:(b + 1) * KB]
        zv = p.dma("sync", zt, z_h.rearrange("(t p) n -> p t n", p=128)[:, tt, :], sq,
                   w=zt_free[b] + ((sq, sq_z),))
        for c in range(DC):
            E("tensor", "transpose", TR_ps[:, c * 128:(c + 1) * 128],
              zt[:, c * 128:(c + 1) * 128], id_bf[:],
              w=(((sq, zv),) + fst["tr"][0] + fst["tr"][1]) if c == 0 else ())
        pe_tr = IL("tensor", pe)
        zt_free[b] = ((pe, pe_tr),)
        E("vector", "tensor_copy", zT[:].rearrange("p (c n) -> p c n", n=128),
          TR_ps[:].rearrange("p (c n) -> p c n", n=128), w=((pe, pe_tr),))
        ve_zT = IL("vector", ve)
        fst["tr"] = [((ve, ve_zT),), ((ve, ve_zT),)]
        for half in range(2):
            bk = (tt % 2) * 2 + half
            v = mm_group(Sb(bk),
                         [zT[:, c * 128:(c + 1) * 128] for c in range(DC)],
                         [wot_sb[:, c * KB + half * 512:c * KB + half * 512 + 512]
                          for c in range(DC)],
                         extra_w=((ve, ve_zT), (gq, gv_wot)) + bank_free[bk])
            E("vector", "tensor_copy", sst[:, half * 512:half * 512 + 512], Sb(bk),
              w=((pe, v),) + (sst_free[b] if half == 0 else ()))
            bank_free[bk] = ((ve, IL("vector", ve)),)
        sv = p.dma("sync", s_h[tt * 128:(tt + 1) * 128, :], sst, sq, w=((ve, ve.n),))
        sst_free[b] = ((sq, sv),)
    sq_p1c = sq.n

    if stage == "s":
        __fence = ((pe, pe.n), (ve, ve.n), (se, se.n))
        _copy_out(p, sq, s_h, out_d, xls_sb, sq_p1c, extra=__fence)
        return _close(nc, p)

    sqf_free = [(), ()]
    aux_free = fst["aux"]
    for ts in range(T // 512):
        for c in range(DC):
            b = c % 2
            sqf = sqf_sb[:, b * 512:(b + 1) * 512]
            E("scalar", "activation", sqf,
              xn_T[:, c * T + ts * 512:c * T + ts * 512 + 512], AF.Square,
              w=sqf_free[b])
            se_sq = IL("scalar", se)
            E("tensor", "matmul", AUX_ps[0:1, 0:512], onesc_sb[:], sqf,
              start=(c == 0), stop=(c == DC - 1),
              w=((se, se_sq), (ve, ve_onesc)) + (aux_free if c == 0 else ()))
            sqf_free[b] = ((pe, IL("tensor", pe)),)
        pe_ss = pe.n
        rrow = rrow_sb[:, (ts % 2) * 512:(ts % 2) * 512 + 512]
        E("scalar", "activation", rrow, AUX_ps[0:1, 0:512], AF.Ln, scale=1.0 / D,
          bias=eps6_sb[0:1, :], w=((pe, pe_ss), (ve, ve_onesc)))
        v = IL("scalar", se)
        E("scalar", "activation", rrow, rrow, AF.Exp, scale=-0.5, w=((se, v),))
        se_r = IL("scalar", se)
        aux_free = ((se, se_r),)
        bk = ts % 2
        E("tensor", "matmul", Sb(bk), onesr_sb[:], rrow, start=True, stop=True,
          w=((se, se_r),) + bank_free[bk])
        pe_bc = IL("tensor", pe)
        for c in range(DC):
            E("vector", "scalar_tensor_tensor",
              nx_T[:, c * T + ts * 512:c * T + ts * 512 + 512],
              xn_T[:, c * T + ts * 512:c * T + ts * 512 + 512],
              g1_sb[:, c:c + 1], Sb(bk), ALU.mult, ALU.mult,
              w=((pe, pe_bc), (sq, sq_g1)) if c == 0 else ())
        bank_free[bk] = ((ve, IL("vector", ve)),)
    fst["aux"] = aux_free
    ve_f1 = ve.n

    if stage == "f1":
        __fence = ((pe, pe.n), (ve, ve.n), (se, se.n))
        _copy_out(p, sq, xn_h, out_d, xls_sb, sq_p1c, extra=__fence)
        return _close(nc, p)

    gv_fwq = p.dma("gpsimd", fwq_sb[:].rearrange("p (c n) -> p c n", c=DC),
                   fwqt_d.rearrange("(c p) n -> p c n", p=128), gq, w=fence())
    gv_fwkc = p.dma("gpsimd", fwkc_sb[:].rearrange("p (c n) -> p c n", c=DC),
                    fwkct_d.rearrange("(c p) n -> p c n", p=128), gq)
    gv_fwvc = p.dma("gpsimd", fwvc_sb[:].rearrange("p (c n) -> p c n", c=DC),
                    fwvct_d.rearrange("(c p) n -> p c n", p=128), gq)
    proj_fm(qt_f[:], 8, fwq_sb[:], KB, nx_T[:], extra_w=((gq, gv_fwq), (ve, ve_f1)))
    proj_fm(kct_sb[:], 2, fwkc_sb[:], 256, nx_T[:],
            extra_w=((gq, gv_fwkc), (sq, sq_conv[0])))
    proj_tm_256(vc_sb[:], fwvc_sb[:], nx_T[:], extra_w=((gq, gv_fwvc),))
    fst["s_free"] = bank_free[0] + bank_free[1] + bank_free[2] + bank_free[3] + \
        fst["s_free"]

    if stage == "f2":
        __fence = ((pe, pe.n), (ve, ve.n), (se, se.n))
        _copy_out(p, sq, xn_h, out_d, xls_sb, sq_p1c, extra=__fence)
        return _close(nc, p)

    for hl in range(4):
        for qt in range(NT):
            flash(L=T, scale=1.0 / 16.0, causal=False,
                  qls=[qt_f[:, (2 * hl + j) * T + qt * 128:(2 * hl + j) * T + qt * 128 + 128]
                       for j in range(2)],
                  k_row=lambda off, n: [kct_sb[:, j * T + off:j * T + off + n]
                                        for j in range(2)],
                  av_rhs=lambda kt: vc_sb[:, kt * 256:(kt + 1) * 256],
                  dh=256,
                  z_dst=zf_sb[:, qt * KB + hl * 256:qt * KB + hl * 256 + 256],
                  z_w=())
    ve_f2b = ve.n
    pe_f2b = pe.n

    if stage == "f2b":
        __fence = ((pe, pe.n), (ve, ve.n), (se, se.n))
        _copy_out(p, sq, xn_h, out_d, xls_sb, sq_p1c, extra=__fence)
        return _close(nc, p)

    gv_fwot = p.dma("gpsimd", fwot_sb[:].rearrange("p (c n) -> p c n", c=DC),
                    fwot_d.rearrange("(c p) n -> p c n", p=128), gq, w=fence())
    for i in range(4):
        bank_free[i] = bank_free[i] + fst["s_free"]
    zft_free = [(), ()]
    x1_free = [(), ()]
    xn_free = [(), ()]
    xff_free = [(), ()]
    aux_r = fst["aux"]
    b23_free = ()
    for tt in range(NT):
        b = tt % 2
        zft = zft_sb[:, b * KB:(b + 1) * KB]
        zfT = zfT_sb[:, b * KB:(b + 1) * KB]
        xnl = xnl_sb[:, b * KB:(b + 1) * KB]
        x1t = x1t_sb[:, b * KB:(b + 1) * KB]
        xff = xff_sb[:, b * KB:(b + 1) * KB]
        for c in range(DC):
            E("tensor", "transpose", TR_ps[:, c * 128:(c + 1) * 128],
              zf_sb[:, tt * KB + c * 128:tt * KB + (c + 1) * 128], id_bf[:],
              w=(((ve, ve_f2b),) + fst["tr"][0] + fst["tr"][1]) if c == 0 else ())
        pe_tr = IL("tensor", pe)
        E("vector", "tensor_copy", zfT[:].rearrange("p (c n) -> p c n", n=128),
          TR_ps[:].rearrange("p (c n) -> p c n", n=128), w=((pe, pe_tr),))
        ve_zfT = IL("vector", ve)
        fst["tr"] = [((ve, ve_zfT),), ((ve, ve_zfT),)]
        for half in range(2):
            mm_group(Sb(half),
                     [zfT[:, c * 128:(c + 1) * 128] for c in range(DC)],
                     [fwot_sb[:, c * KB + half * 512:c * KB + half * 512 + 512]
                      for c in range(DC)],
                     extra_w=((ve, ve_zfT), (gq, gv_fwot)) + bank_free[half])
        pe_ao = pe.n
        xnv = p.dma("sync", xnl, xn_h[tt * 128:(tt + 1) * 128, :], sq,
                    w=xn_free[b] + ((sq, sq_p0), (pe, pe_f2b)))
        E("vector", "tensor_add", x1t[:, 0:512], xnl[:, 0:512], Sb(0),
          w=((pe, pe_ao), (sq, xnv)) + x1_free[b])
        E("vector", "tensor_add", x1t[:, 512:1024], xnl[:, 512:1024], Sb(1))
        ve_x1 = IL("vector", ve)
        xn_free[b] = ((ve, ve_x1),)
        bank_free[0] = ((ve, ve_x1),)
        bank_free[1] = ((ve, ve_x1),)
        x1v = p.dma("sync", x1_h[tt * 128:(tt + 1) * 128, :], x1t, sq, w=((ve, ve_x1),))
        if "norm2" in _F2C_SKIP:
            continue
        E("scalar", "activation", sqd_sb[:], x1t, AF.Square, accum_out=ssum_sb[:],
          w=((ve, ve_x1),))
        v = IL("scalar", se)
        E("scalar", "activation", ssum_sb[:], ssum_sb[:], AF.Ln, scale=1.0 / D,
          bias=eps6_sb[:], w=((se, v), (ve, ve_onesc)))
        v = IL("scalar", se)
        E("scalar", "activation", rl_sb[:], ssum_sb[:], AF.Exp, scale=-0.5, w=((se, v),))
        se_r2 = IL("scalar", se)
        E("vector", "scalar_tensor_tensor", xff, x1t, rl_sb[:], g2b_sb[:],
          ALU.mult, ALU.mult, w=((se, se_r2), (ve, ve_bcast)) + xff_free[b])
        ve_xf = IL("vector", ve)
        x1_free[b] = ((sq, x1v), (ve, ve_xf))
        if "xft" in _F2C_SKIP:
            continue
        for c in range(DC):
            E("tensor", "transpose", S_ps[:, 1024 + c * 128:1024 + c * 128 + 128],
              xff[:, c * 128:(c + 1) * 128], idf_sb[:],
              w=(((ve, ve_xf), (sq, sq_idf)) + bank_free[2] + bank_free[3] + b23_free)
                if c == 0 else ())
        pe_xt = IL("tensor", pe)
        xff_free[b] = ((pe, pe_xt),)
        if "xfc" in _F2C_SKIP:
            E("vector", "engine_nop", w=((pe, pe_xt),))
            ve_xfT = IL("vector", ve)
            E("scalar", "nop", w=((pe, pe_xt),))
            se_xfT = IL("scalar", se)
            b23_free = ((ve, ve_xfT), (se, se_xfT))
            bank_free[2] = ((ve, ve_xfT),)
            bank_free[3] = ((se, se_xfT),)
            continue
        E("vector", "tensor_copy", xfTf_sb[:].rearrange("p (c n) -> p c n", n=128),
          S_ps[:, 1024:2048].rearrange("p (c n) -> p c n", n=128), w=((pe, pe_xt),))
        ve_xfT = IL("vector", ve)
        E("scalar", "activation",
          xf_T[:].rearrange("p (c n) -> p c n", n=T)[:, :, tt * 128:(tt + 1) * 128],
          S_ps[:, 1024:2048].rearrange("p (c n) -> p c n", n=128), AF.Copy,
          w=((pe, pe_xt), (ve, ve_xfT)))
        se_xfT = IL("scalar", se)
        b23_free = ((ve, ve_xfT), (se, se_xfT))
        bank_free[2] = ((ve, ve_xfT),)
        bank_free[3] = ((se, se_xfT),)
        if "router" in _F2C_SKIP:
            continue
        v = mm_group(AUX_ps[:, 0:8],
                     [xfTf_sb[:, c * 128:(c + 1) * 128] for c in range(DC)],
                     [wrt_sb[:, c * 8:(c + 1) * 8] for c in range(DC)],
                     extra_w=((ve, ve_xfT), (sq, sq_wrt)) + aux_r)
        E("vector", "tensor_add", lg_sb[:], AUX_ps[:, 0:8], ebb_sb[:],
          w=((pe, v), (ve, ve_bcast)))
        vv = IL("vector", ve)
        aux_r = ((ve, vv),)
        E("vector", "tensor_reduce", m1_sb[:], lg_sb[:], AX.X, ALU.max, negate=True,
          w=((ve, vv),))
        vv = IL("vector", ve)
        E("scalar", "activation", rwu_sb[:], lg_sb[:], AF.Exp, bias=m1_sb[:],
          accum_out=den_sb[:], w=((ve, vv),))
        sv = IL("scalar", se)
        E("vector", "reciprocal", den_sb[:], den_sb[:], w=((se, sv),))
        vv = IL("vector", ve)
        E("vector", "tensor_scalar_mul", rw_sb[:], rwu_sb[:], den_sb[:], w=((ve, vv),))
        vv = IL("vector", ve)
        E("vector", "tensor_reduce", m1_sb[:], rw_sb[:], AX.X, ALU.max, w=((ve, vv),))
        vv = IL("vector", ve)
        E("vector", "tensor_single_scalar", eq1_sb[:], rw_sb[:], m1_sb[:],
          ALU.is_equal, w=((ve, vv),))
        vv = IL("vector", ve)
        E("vector", "scalar_tensor_tensor", rw2_sb[:], eq1_sb[:], -2.0, rw_sb[:],
          ALU.mult, ALU.add, w=((ve, vv),))
        vv = IL("vector", ve)
        E("vector", "tensor_reduce", m2_sb[:], rw2_sb[:], AX.X, ALU.max, w=((ve, vv),))
        vv = IL("vector", ve)
        E("vector", "tensor_single_scalar", eq2_sb[:], rw2_sb[:], m2_sb[:],
          ALU.is_equal, w=((ve, vv),))
        E("vector", "tensor_copy", e12_sb[:, 0:1], m1_sb[:])
        E("vector", "tensor_copy", e12_sb[:, 1:2], m2_sb[:])
        vv = IL("vector", ve)
        E("scalar", "activation", e12_sb[:], e12_sb[:], AF.Exp, w=((ve, vv),))
        sv = IL("scalar", se)
        E("vector", "reduce_sum", den_sb[:], e12_sb[:], AX.X, w=((se, sv),))
        vv = IL("vector", ve)
        E("vector", "reciprocal", den_sb[:], den_sb[:], w=((ve, vv),))
        vv = IL("vector", ve)
        E("vector", "tensor_scalar_mul", w12_sb[:], e12_sb[:], den_sb[:], w=((ve, vv),))
        vv = IL("vector", ve)
        E("vector", "tensor_scalar_mul", eq1_sb[:], eq1_sb[:], w12_sb[:, 0:1],
          w=((ve, vv),))
        vv = IL("vector", ve)
        E("vector", "scalar_tensor_tensor", cmb_sb[:, tt * 8:(tt + 1) * 8],
          eq2_sb[:], w12_sb[:, 1:2], eq1_sb[:], ALU.mult, ALU.add, w=((ve, vv),))
        IL("vector", ve)
    sq_f2c = sq.n
    se_f2c = se.n

    if stage == "x1":
        __fence = ((pe, pe.n), (ve, ve.n), (se, se.n))
        _copy_out(p, sq, x1_h, out_d, xls_sb, sq_f2c, extra=__fence)
        return _close(nc, p)

    NSLAB = 4
    SL = T // NSLAB
    pe_f2c = pe.n
    fw3 = fence()
    ring_free = [fw3, fw3, fw3]
    hb_free = {i: bank_free[i] for i in range(4)}
    eob_free = {0: fst["aux"], 1: ()}
    tmps_free = ()
    moe_free = ((se, se_f2c),)
    prev_ve_h = 0
    prev_pe_eo = 0
    for s in range(NSLAB):
        ve_moez = 0
        for e in range(9):
            mi = (3 + e * 3) if e < 8 else 0
            wvs = []
            for k in range(3):
                wvs.append(p.dma("gpsimd", wring[:, k * 8 * KB:(k + 1) * 8 * KB],
                                 ewbf_h[mi + k], gq, w=ring_free[k]))
            w1b = wring[:, 0:8 * KB]
            w3b = wring[:, 8 * KB:16 * KB]
            w2b = wring[:, 16 * KB:24 * KB]
            for fc in range(DC):
                for th in range(2):
                    bk = (fc * 2 + th) % 2
                    v = mm_group(
                        Sb(bk),
                        [w3b[:, c * KB + fc * 128:c * KB + fc * 128 + 128]
                         for c in range(DC)],
                        [xf_T[:, c * T + s * SL + th * 512:c * T + s * SL + th * 512 + 512]
                         for c in range(DC)],
                        extra_w=hb_free[bk] + ((gq, wvs[1]), (se, se_f2c),
                                               (ve, prev_ve_h)))
                    E("scalar", "activation",
                      h3_sb[:, fc * KB + th * 512:fc * KB + th * 512 + 512], Sb(bk),
                      AF.Copy, w=((pe, v),))
                    hb_free[bk] = ((se, IL("scalar", se)),)
            se_h3 = se.n
            for fc in range(DC):
                for th in range(2):
                    bk = 2 + (fc * 2 + th) % 2
                    v = mm_group(
                        Sb(bk),
                        [w1b[:, c * KB + fc * 128:c * KB + fc * 128 + 128]
                         for c in range(DC)],
                        [xf_T[:, c * T + s * SL + th * 512:c * T + s * SL + th * 512 + 512]
                         for c in range(DC)],
                        extra_w=hb_free[bk] + ((gq, wvs[0]),))
                    E("scalar", "activation", tmps_sb[:], Sb(bk), AF.Silu,
                      w=((pe, v),) + tmps_free)
                    sv = IL("scalar", se)
                    E("vector", "tensor_mul",
                      h_sb[:, fc * KB + th * 512:fc * KB + th * 512 + 512],
                      tmps_sb[:], h3_sb[:, fc * KB + th * 512:fc * KB + th * 512 + 512],
                      w=((se, sv), (se, se_h3), (pe, prev_pe_eo)))
                    vv = IL("vector", ve)
                    hb_free[bk] = ((ve, vv),)
                    tmps_free = ((ve, vv),)
            ve_h = ve.n
            pe_h1 = pe.n
            for tl in range(SL // 128):
                for dhh in range(2):
                    eops = AUX_ps if dhh == 0 else SPR_ps
                    v = mm_group(
                        eops[:, 0:512],
                        [h_sb[:, fc * KB + tl * 128:fc * KB + tl * 128 + 128]
                         for fc in range(DC)],
                        [w2b[:, fc * KB + dhh * 512:fc * KB + dhh * 512 + 512]
                         for fc in range(DC)],
                        extra_w=((ve, ve_h), (gq, wvs[2])) + eob_free[dhh])
                    dst = moe_sb[:, tl * KB + dhh * 512:tl * KB + dhh * 512 + 512]
                    scl = cmb_sb[:, (s * 8 + tl) * 8 + e:(s * 8 + tl) * 8 + e + 1] \
                        if e < 8 else 1.0
                    if e == 0:
                        E("vector", "tensor_scalar_mul", dst, eops[:, 0:512], scl,
                          w=((pe, v),) + moe_free)
                    else:
                        E("vector", "scalar_tensor_tensor", dst, eops[:, 0:512], scl,
                          dst, ALU.mult, ALU.add, w=((pe, v),))
                    eob_free[dhh] = ((ve, IL("vector", ve)),)
            ring_free[0] = ((pe, pe_h1),)
            ring_free[1] = ((pe, pe_h1),)
            ring_free[2] = ((pe, pe.n),)
            prev_ve_h = ve_h
            prev_pe_eo = pe.n
        ve_moe = ve.n
        xl_free = sl_free = x1l_free = ()
        prev_store = ()
        for tl in range(SL // 128):
            tt = s * (SL // 128) + tl
            xv = p.dma("sync", xls_sb[:], x_d[tt * 128:(tt + 1) * 128, :], sq,
                       w=xl_free + ((pe, pe_f2c),))
            sv = p.dma("sync", sls_sb[:], s_h[tt * 128:(tt + 1) * 128, :], sq,
                       w=sl_free + ((sq, sq_p1c),))
            x1v = p.dma("sync", x1l_sb[:], x1_h[tt * 128:(tt + 1) * 128, :], sq,
                        w=x1l_free + ((sq, sq_f2c),))
            E("vector", "tensor_add", gl_sb[:], x1l_sb[:],
              moe_sb[:, tl * KB:(tl + 1) * KB], w=((sq, x1v), (ve, ve_moe)))
            x1l_free = ((ve, IL("vector", ve)),)
            E("scalar", "activation", gl_sb[:], gl_sb[:], AF.Silu, w=((ve, ve.n),))
            sv2 = IL("scalar", se)
            E("vector", "tensor_mul", gl_sb[:], gl_sb[:], sls_sb[:],
              w=((se, sv2), (sq, sv)))
            sl_free = ((ve, IL("vector", ve)),)
            E("vector", "tensor_add", outt_sb[:], gl_sb[:], xls_sb[:],
              w=((ve, ve.n), (sq, xv)) + prev_store)
            xl_free = ((ve, IL("vector", ve)),)
            ov = p.dma("sync", out_d[tt * 128:(tt + 1) * 128, :], outt_sb[:], sq,
                       w=((ve, ve.n),))
            prev_store = ((sq, ov),)
        moe_free = ((ve, ve.n),)

    return _close(nc, p)


def _copy_out(p, sq, src_h, out_d, tile_sb, sq_src, extra=()):
    last = ((sq, sq_src),) + extra
    for tt in range(NT):
        v = p.dma("sync", tile_sb[:], src_h[tt * 128:(tt + 1) * 128, :], sq, w=last)
        ov = p.dma("sync", out_d[tt * 128:(tt + 1) * 128, :], tile_sb[:], sq,
                   w=((sq, v),))
        last = ((sq, ov),)


def _close(nc, p):
    sems = {s.name: s for s in p.sems}
    p.emit("sync", "nop", w=((sems["sq"], sems["sq"].n), (sems["gq"], sems["gq"].n)))
    p.emit("gpsimd", "nop", w=((sems["gq"], sems["gq"].n),))
    with contextlib.ExitStack() as stack:
        handles = {}
        for s in p.sems:
            handles[s] = stack.enter_context(nc.semaphore(s.name))
        block = stack.enter_context(nc.Block())
        for eng in Prog.ENGINES:
            def make(eng_name):
                def f(eng_obj):
                    for meth, args, kw, waits, incs in p.ops[eng_name]:
                        for sm, v in waits:
                            eng_obj.wait_ge(handles[sm], v)
                        ins = getattr(eng_obj, meth)(*args, **kw)
                        for sm, k in incs:
                            ins = ins.then_inc(handles[sm], k)
                return f
            getattr(block, eng)(make(eng))
    return nc


def _prep_inputs(inp):
    f32 = lambda a: np.ascontiguousarray(np.asarray(a, dtype=np.float32))
    tr = lambda a: np.ascontiguousarray(np.asarray(a, dtype=np.float32).T)
    ews = np.empty((27, D, D), np.float32)
    ews[0] = np.asarray(inp["L0_ffn_sW1"], np.float32).T
    ews[1] = np.asarray(inp["L0_ffn_sW3"], np.float32).T
    ews[2] = np.asarray(inp["L0_ffn_sW2"], np.float32).T
    eW1 = np.asarray(inp["L0_ffn_eW1"], np.float32)
    eW2 = np.asarray(inp["L0_ffn_eW2"], np.float32)
    eW3 = np.asarray(inp["L0_ffn_eW3"], np.float32)
    for e in range(8):
        ews[3 + 3 * e + 0] = eW1[e].T
        ews[3 + 3 * e + 1] = eW3[e].T
        ews[3 + 3 * e + 2] = eW2[e].T
    mask = np.where(np.arange(128)[None, :] > np.arange(128)[:, None],
                    np.float32(-1e10), np.float32(0.0)).astype(np.float32)
    common = dict(
        wkt=tr(np.asarray(inp["L0_attn_W_K"], np.float32).reshape(D, D)),
        wqt=tr(np.asarray(inp["L0_attn_W_Q"], np.float32).reshape(D, D)),
        wvt=tr(np.asarray(inp["L0_attn_W_V"], np.float32).reshape(D, D)),
        wot=tr(inp["L0_attn_W_O"]),
        fwqt=tr(inp["L0_ffn_Wq"]), fwkct=tr(inp["L0_ffn_Wkc"]),
        fwvct=tr(inp["L0_ffn_Wvc"]), fwot=tr(inp["L0_ffn_Wo"]),
        fwrt=tr(inp["L0_ffn_Wr"]), ews=np.ascontiguousarray(ews),
        g1=f32(inp["L0_ffn_g1"]), g2=f32(inp["L0_ffn_g2"]),
        eb=f32(inp["L0_ffn_expert_bias"]),
        ident=np.eye(128, dtype=np.float32), mask=np.ascontiguousarray(mask),
        ones=np.ones((1, 128), np.float32),
    )
    x = np.asarray(inp["L0_attn_x"], np.float32)
    return [dict(common, x=np.ascontiguousarray(x[b])) for b in range(8)]


def kernel(**inputs):
    in_maps = _prep_inputs(inputs)
    nc = build(_STAGE)
    res = run_bass_kernel_spmd(nc, in_maps, core_ids=list(range(8)))
    return np.stack([np.asarray(r["out"]) for r in res.results], axis=0)


# version 18
# speedup vs baseline: 1.0177x; 1.0013x over previous
import contextlib
import numpy as np
import concourse.bass as bass
import concourse.mybir as mybir
from concourse.bass_utils import run_bass_kernel_spmd

F32 = mybir.dt.float32
BF16 = mybir.dt.bfloat16
AF = mybir.ActivationFunctionType
ALU = mybir.AluOpType
AX = mybir.AxisListType

T = 4096
D = 1024
NT = T // 128
DC = D // 128
KB = 1024

_STAGE = "full"
_DBG_SKIP = ()
_NG, _NH, _NQ, _Q0 = 4, 4, 32, 0
_FL_SKIP = ()
_F2C_SKIP = ()


class Sem:
    __slots__ = ("name", "n")

    def __init__(self, name):
        self.name = name
        self.n = 0


class Prog:
    ENGINES = ("tensor", "vector", "scalar", "sync", "gpsimd")

    def __init__(self):
        self.ops = {e: [] for e in self.ENGINES}
        self.sems = []
        self._waited = {e: {} for e in self.ENGINES}

    def sem(self, name):
        s = Sem(name)
        self.sems.append(s)
        return s

    def emit(self, eng, meth, *args, w=(), **kw):
        ww = []
        for s, v in w:
            if v > 0 and self._waited[eng].get(s, 0) < v:
                self._waited[eng][s] = v
                ww.append((s, v))
        self.ops[eng].append([meth, args, kw, tuple(ww), ()])

    def inc_last(self, eng, sem, k=1):
        sem.n += k
        op = self.ops[eng][-1]
        op[4] = op[4] + ((sem, k),)
        return sem.n

    def dma(self, eng, out, in_, sem, w=(), **kw):
        self.emit(eng, "dma_start", out, in_, w=w, **kw)
        return self.inc_last(eng, sem, 16)


def build(stage="full"):
    skip = _DBG_SKIP
    nc = bass.Bass("TRN2", target_bir_lowering=False, debug=False)

    x_d = nc.dram_tensor("x", [T, D], F32, kind="ExternalInput").ap()
    wkt_d = nc.dram_tensor("wkt", [D, D], F32, kind="ExternalInput").ap()
    wqt_d = nc.dram_tensor("wqt", [D, D], F32, kind="ExternalInput").ap()
    wvt_d = nc.dram_tensor("wvt", [D, D], F32, kind="ExternalInput").ap()
    wot_d = nc.dram_tensor("wot", [D, D], F32, kind="ExternalInput").ap()
    fwqt_d = nc.dram_tensor("fwqt", [D, D], F32, kind="ExternalInput").ap()
    fwkct_d = nc.dram_tensor("fwkct", [D, 256], F32, kind="ExternalInput").ap()
    fwvct_d = nc.dram_tensor("fwvct", [D, 256], F32, kind="ExternalInput").ap()
    fwot_d = nc.dram_tensor("fwot", [D, D], F32, kind="ExternalInput").ap()
    fwrt_d = nc.dram_tensor("fwrt", [D, 8], F32, kind="ExternalInput").ap()
    ews_d = nc.dram_tensor("ews", [27, D, D], F32, kind="ExternalInput").ap()
    g1_d = nc.dram_tensor("g1", [D], F32, kind="ExternalInput").ap()
    g2_d = nc.dram_tensor("g2", [D], F32, kind="ExternalInput").ap()
    eb_d = nc.dram_tensor("eb", [8], F32, kind="ExternalInput").ap()
    ident_d = nc.dram_tensor("ident", [128, 128], F32, kind="ExternalInput").ap()
    mask_d = nc.dram_tensor("mask", [128, 128], F32, kind="ExternalInput").ap()
    ones_d = nc.dram_tensor("ones", [1, 128], F32, kind="ExternalInput").ap()
    out_d = nc.dram_tensor("out", [T, D], F32, kind="ExternalOutput").ap()
    dbg_d = nc.dram_tensor("dbg", [128, NT * 4], F32, kind="ExternalOutput").ap() \
        if stage == "xn" else None

    xn_h = nc.dram_tensor("xn_h", [T, D], F32, kind="Internal").ap()
    s_h = nc.dram_tensor("s_h", [T, D], F32, kind="Internal").ap()
    x1_h = nc.dram_tensor("x1_h", [T, D], F32, kind="Internal").ap()
    z_h = nc.dram_tensor("z_h", [T, D], BF16, kind="Internal").ap()
    ewbf_h = nc.dram_tensor("ewbf", [27, 128, 8 * KB], BF16, kind="Internal").ap()

    def sb(name, shape, dtype, off):
        return nc.alloc_sbuf_tensor_at(name, shape, dtype, offset=off + 16 * KB + 128)

    xn_T = sb("xn_T", [128, 8 * T], BF16, 0)
    qt_f = sb("qt_f", [128, 8 * T], BF16, 0)
    xf_T = sb("xf_T", [128, 8 * T], BF16, 0)
    nx_T = sb("nx_T", [128, 8 * T], BF16, 64 * KB)
    zf_sb = sb("zf_sb", [128, NT * KB], BF16, 64 * KB)
    kt_g = sb("kt_g", [128, 2 * T], BF16, 64 * KB)
    qt_g = sb("qt_g", [128, 2 * T], BF16, 80 * KB)
    v_g = sb("v_g", [128, NT * 256], BF16, 96 * KB)
    zg_sb = sb("zg_sb", [128, NT * 256], BF16, 112 * KB)
    h3_sb = sb("h3_sb", [128, 8 * KB], BF16, 64 * KB)
    h_sb = sb("h_sb", [128, 8 * KB], BF16, 80 * KB)
    moe_sb = sb("moe_sb", [128, 8 * KB], F32, 96 * KB)

    xt_sb = sb("xt_sb", [128, 2 * KB], F32, 128 * KB)
    xnf_sb = sb("xnf_sb", [128, 2 * KB], F32, 136 * KB)
    xnb_sb = sb("xnb_sb", [128, 2 * KB], BF16, 144 * KB)

    wk_sb = sb("wk_sb", [128, 2 * KB], BF16, 128 * KB)
    wq_sb = sb("wq_sb", [128, 2 * KB], BF16, 132 * KB)
    wv_sb = sb("wv_sb", [128, 2 * KB], BF16, 136 * KB)
    p_sb = sb("p_sb", [128, 3 * KB], BF16, 140 * KB)
    pt_sb = sb("pt_sb", [128, 3 * KB], BF16, 146 * KB)
    stg_sb = sb("stg_sb", [128, 8 * KB], BF16, 152 * KB)

    wot_sb = sb("wot_sb", [128, 8 * KB], BF16, 168 * KB)
    zt_sb = sb("zt_sb", [128, 2 * KB], BF16, 184 * KB)
    zT_sb = sb("zT_sb", [128, 2 * KB], BF16, 188 * KB)
    sst_sb = sb("sst_sb", [128, 2 * KB], F32, 128 * KB)

    sqf_sb = sb("sqf_sb", [128, 2 * 512], F32, 136 * KB)
    rrow_sb = sb("rrow_sb", [1, 2 * 512], F32, 140 * KB)
    g2r_sb = sb("g2r_sb", [1, KB], F32, 160 * KB)
    ebr_sb = sb("ebr_sb", [1, 8], F32, 164 * KB)

    fwq_sb = sb("fwq_sb", [128, 8 * KB], BF16, 128 * KB)
    fwkc_sb = sb("fwkc_sb", [128, 2 * KB], BF16, 144 * KB)
    fwvc_sb = sb("fwvc_sb", [128, 2 * KB], BF16, 148 * KB)
    kct_sb = sb("kct_sb", [128, 2 * T], BF16, 152 * KB)
    vc_sb = sb("vc_sb", [128, NT * 256], BF16, 168 * KB)

    fwot_sb = sb("fwot_sb", [128, 8 * KB], BF16, 128 * KB)
    zft_sb = sb("zft_sb", [128, 2 * KB], BF16, 144 * KB)
    zfT_sb = sb("zfT_sb", [128, 2 * KB], BF16, 148 * KB)
    xnl_sb = sb("xnl_sb", [128, 2 * KB], F32, 152 * KB)
    x1t_sb = sb("x1t_sb", [128, 2 * KB], F32, 160 * KB)
    xff_sb = sb("xff_sb", [128, 2 * KB], F32, 168 * KB)
    xfTf_sb = sb("xfTf_sb", [128, KB], F32, 176 * KB)

    wring = sb("wring", [128, 3 * 8 * KB], BF16, 128 * KB)
    xls_sb = sb("xls_sb", [128, KB], F32, 176 * KB)
    sls_sb = sb("sls_sb", [128, KB], F32, 180 * KB)
    x1l_sb = sb("x1l_sb", [128, KB], F32, 184 * KB)
    gl_sb = sb("gl_sb", [128, KB], F32, 188 * KB)
    tmps_sb = sb("tmps_sb", [128, 512], BF16, 192 * KB)
    outt_sb = sb("outt_sb", [128, KB], F32, 193 * KB)

    sqd_sb = sb("sqd_sb", [128, KB], BF16, 197 * KB)
    id_bf = sb("id_bf", [128, 128], BF16, 199 * KB)
    idf_sb = sb("idf_sb", [128, 128], F32, 199 * KB + 256)
    mask_sb = sb("mask_sb", [128, 128], F32, 199 * KB + 768)
    g2b_sb = sb("g2b_sb", [128, KB], F32, 200 * KB + 256)
    cmb_sb = sb("cmb_sb", [128, NT * 8], F32, 204 * KB + 256)
    onesr_sb = sb("onesr_sb", [1, 128], F32, 205 * KB + 256)
    onesc_sb = sb("onesc_sb", [128, 1], F32, 205 * KB + 768)
    g1_sb = sb("g1_sb", [128, 8], F32, 205 * KB + 800)
    wrt_sb = sb("wrt_sb", [128, 64], F32, 205 * KB + 832)
    ebb_sb = sb("ebb_sb", [128, 8], F32, 205 * KB + 1088)
    stoff = 206 * KB + 256
    nm_sb = sb("nm_sb", [128, 1], F32, stoff + 0)
    nm2_sb = sb("nm2_sb", [128, 1], F32, stoff + 32)
    pm_sb = sb("pm_sb", [128, 1], F32, stoff + 64)
    a0_sb = sb("a0_sb", [128, 1], F32, stoff + 96)
    l0_sb = sb("l0_sb", [128, 1], F32, stoff + 128)
    l1_sb = sb("l1_sb", [128, 1], F32, stoff + 160)
    rl_sb = sb("rl_sb", [128, 1], F32, stoff + 192)
    ssum_sb = sb("ssum_sb", [128, 1], F32, stoff + 224)
    osb_sb = sb("osb_sb", [128, 256], F32, stoff + 256)
    lg_sb = sb("lg_sb", [128, 8], F32, stoff + 1280)
    rwu_sb = sb("rwu_sb", [128, 8], F32, stoff + 1312)
    rw_sb = sb("rw_sb", [128, 8], F32, stoff + 1344)
    eq1_sb = sb("eq1_sb", [128, 8], F32, stoff + 1376)
    eq2_sb = sb("eq2_sb", [128, 8], F32, stoff + 1408)
    rw2_sb = sb("rw2_sb", [128, 8], F32, stoff + 1440)
    e12_sb = sb("e12_sb", [128, 2], F32, stoff + 1472)
    w12_sb = sb("w12_sb", [128, 2], F32, stoff + 1504)
    den_sb = sb("den_sb", [128, 1], F32, stoff + 1536)
    m1_sb = sb("m1_sb", [128, 1], F32, stoff + 1568)
    m2_sb = sb("m2_sb", [128, 1], F32, stoff + 1600)
    l2_sb = sb("l2_sb", [128, 1], F32, stoff + 1632)
    eps5_sb = sb("eps5_sb", [128, 1], F32, 205 * KB + 1152)
    eps6_sb = sb("eps6_sb", [128, 1], F32, 205 * KB + 1184)
    dbg_sb = sb("dbg_sb", [128, NT * 4], F32, 188 * KB) if stage == "xn" else None

    S_ps = nc.alloc_psum_tensor("S_ps", [128, 4 * 512], F32)
    TRA_ps = nc.alloc_psum_tensor("TRA_ps", [128, 1024], BF16)
    TRB_ps = nc.alloc_psum_tensor("TRB_ps", [128, 1024], BF16)
    AUX_ps = nc.alloc_psum_tensor("AUX_ps", [128, 512], F32)
    SPR_ps = nc.alloc_psum_tensor("SPR_ps", [128, 512], F32)
    TR_ps = TRA_ps

    def Sb(i):
        return S_ps[:, i * 512:(i + 1) * 512]

    p = Prog()
    pe, ve, se, sq, gq = (p.sem(n) for n in ("pe", "ve", "se", "sq", "gq"))
    E = p.emit
    IL = p.inc_last

    gq_id = p.dma("gpsimd", id_bf[:], ident_d, gq)
    sq_mask = p.dma("sync", mask_sb[:], mask_d, sq)
    sq_idf = p.dma("sync", idf_sb[:], ident_d, sq)
    sq_ones = p.dma("sync", onesr_sb[:], ones_d, sq)
    sq_g1 = p.dma("sync", g1_sb[:], g1_d.rearrange("(c p) -> p c", p=128), sq,
                  allow_slow_non_contiguous=True)
    sq_wrt = p.dma("sync", wrt_sb[:].rearrange("p (c n) -> p c n", c=8),
                   fwrt_d.rearrange("(c p) n -> p c n", p=128), sq)
    sq_g2r = p.dma("sync", g2r_sb[:], g2_d.rearrange("(a n) -> a n", a=1), sq)
    sq_ebr = p.dma("sync", ebr_sb[:], eb_d.rearrange("(a n) -> a n", a=1), sq)
    E("vector", "memset", onesc_sb[:], 1.0)
    E("vector", "memset", eps5_sb[:], 1e-5)
    E("vector", "memset", eps6_sb[:], 1e-6)
    ve_onesc = IL("vector", ve)

    skip = _DBG_SKIP
    if "bcast" not in skip:
        E("tensor", "matmul", Sb(0), onesr_sb[:], g2r_sb[:, 0:512], start=True, stop=True,
          w=((sq, sq_g2r), (sq, sq_ones)))
    if "bcast" not in skip:
        E("tensor", "matmul", Sb(1), onesr_sb[:], g2r_sb[:, 512:1024], start=True, stop=True)
        pe_g2b = IL("tensor", pe)
        E("tensor", "matmul", AUX_ps[:, 0:8], onesr_sb[:], ebr_sb[:], start=True, stop=True,
          w=((sq, sq_ebr),))
        pe_ebb = IL("tensor", pe)
        E("vector", "tensor_copy", g2b_sb[:, 0:512], Sb(0), w=((pe, pe_g2b),))
        E("vector", "tensor_copy", g2b_sb[:, 512:1024], Sb(1))
        E("vector", "tensor_copy", ebb_sb[:], AUX_ps[:, 0:8], w=((pe, pe_ebb),))
    else:
        E("vector", "engine_nop")
    ve_bcast = IL("vector", ve)

    fst = {
        "s_free": ((ve, ve_bcast),),
        "aux": ((ve, ve_bcast),),
        "tr": [((ve, 0),), ((ve, 0),)],
        "p_pe": 0,
        "pt_pe": 0,
    }
    bank_free = {i: ((ve, ve_bcast),) for i in range(4)}

    xt_free = [(), ()]
    xnf_free = [(), ()]
    xnb_free = [(), ()]
    se_sq_prev = 0
    for tt in range(NT):
        b = tt % 2
        xt = xt_sb[:, b * KB:(b + 1) * KB]
        xnf = xnf_sb[:, b * KB:(b + 1) * KB]
        xnb = xnb_sb[:, b * KB:(b + 1) * KB]
        xv = p.dma("sync", xt, x_d[tt * 128:(tt + 1) * 128, :], sq, w=xt_free[b])
        E("vector", "tensor_reduce", nm_sb[:], xt, AX.X, ALU.add, negate=True,
          w=((sq, xv), (se, se_sq_prev)))
        v = IL("vector", ve)
        E("vector", "tensor_scalar_mul", nm_sb[:], nm_sb[:], 1.0 / D, w=((ve, v),))
        ve_nm = IL("vector", ve)
        E("scalar", "activation", sqd_sb[:], xt, AF.Square, bias=nm_sb[:],
          accum_out=ssum_sb[:], w=((ve, ve_nm),))
        v = IL("scalar", se)
        se_sq_prev = v
        E("scalar", "activation", ssum_sb[:], ssum_sb[:], AF.Ln, scale=1.0 / D,
          bias=eps5_sb[:], w=((se, v), (ve, ve_onesc)))
        v = IL("scalar", se)
        E("scalar", "activation", rl_sb[:], ssum_sb[:], AF.Exp, scale=-0.5, w=((se, v),))
        se_st = IL("scalar", se)
        E("vector", "tensor_scalar", xnf, xt, nm_sb[:], rl_sb[:], ALU.add, ALU.mult,
          w=((se, se_st),) + xnf_free[b])
        ve_xnf = IL("vector", ve)
        xt_free[b] = ((ve, ve_xnf),)
        sv = p.dma("sync", xn_h[tt * 128:(tt + 1) * 128, :], xnf, sq, w=((ve, ve_xnf),))
        if "cast" not in skip:
            E("vector", "tensor_copy", xnb, xnf, w=((ve, ve_xnf),) + xnb_free[b])
            ve_cast = IL("vector", ve)
            xnf_free[b] = ((ve, ve_cast), (sq, sv))
        else:
            ve_cast = 0
            xnf_free[b] = ((sq, sv),)
        if "tr" not in skip:
            for c in range(DC):
                E("tensor", "transpose", TR_ps[:, c * 128:(c + 1) * 128],
                  xnb[:, c * 128:(c + 1) * 128], id_bf[:],
                  w=((ve, ve_cast), (gq, gq_id)) + fst["tr"][0] + fst["tr"][1]
                    if c == 0 else ())
            pe_tr = IL("tensor", pe)
            xnb_free[b] = ((pe, pe_tr),)
            for c in range(DC):
                E("vector", "tensor_copy",
                  xn_T[:, c * T + tt * 128:c * T + tt * 128 + 128],
                  TR_ps[:, c * 128:(c + 1) * 128],
                  w=((pe, pe_tr),) if c == 0 else ())
            v = IL("vector", ve)
            fst["tr"] = [((ve, v),), ((ve, v),)]
        if stage == "xn":
            E("vector", "tensor_copy", dbg_sb[:, tt * 4:tt * 4 + 1], nm_sb[:],
              w=((ve, ve_xnf),))
            E("vector", "tensor_copy", dbg_sb[:, tt * 4 + 1:tt * 4 + 2], ssum_sb[:])
            E("vector", "tensor_copy", dbg_sb[:, tt * 4 + 2:tt * 4 + 3], rl_sb[:])
            E("vector", "tensor_copy", dbg_sb[:, tt * 4 + 3:tt * 4 + 4], eps5_sb[:])
            IL("vector", ve)
    sq_p0 = sq.n

    if stage == "xn":
        p.dma("sync", dbg_d, dbg_sb[:], sq, w=((ve, ve.n),))
        _copy_out(p, sq, xn_h, out_d, xls_sb, sq_p0, extra=__fence)
        return _close(nc, p)

    sq_conv = [0]
    def emit_conversions():
        stg_free = [(), ()]
        for m in range(27):
            for hh in range(2):
                b = (m * 2 + hh) % 2
                stg = stg_sb[:, b * 4 * KB:(b + 1) * 4 * KB]
                src = ews_d[m].rearrange("(c p) n -> p c n", p=128)[:, hh * 4:(hh + 1) * 4, :]
                gv = p.dma("gpsimd", stg[:].rearrange("p (c n) -> p c n", c=4), src, gq,
                           w=stg_free[b])
                sv = p.dma("sync", ewbf_h[m][:, hh * 4 * KB:(hh + 1) * 4 * KB], stg, sq,
                           w=((gq, gv),))
                stg_free[b] = ((sq, sv),)

    def fence():
        return ((pe, pe.n), (ve, ve.n), (se, se.n), (sq, sq.n))

    def mm_group(dst_ps, lhsT_list, rhs_list, extra_w=()):
        n = len(lhsT_list)
        for i, (lh, rh) in enumerate(zip(lhsT_list, rhs_list)):
            E("tensor", "matmul", dst_ps, lh, rh, start=(i == 0), stop=(i == n - 1),
              w=extra_w if i == 0 else ())
        return IL("tensor", pe)

    def proj_fm(out_sb, out_cols, w_sb, w_cols, act_T, extra_w=()):
        first = True
        for oc in range(out_cols):
            for ts in range(T // 512):
                bkk = (oc * (T // 512) + ts) % 4
                v = mm_group(
                    Sb(bkk),
                    [w_sb[:, c * w_cols + oc * 128:c * w_cols + oc * 128 + 128]
                     for c in range(DC)],
                    [act_T[:, c * T + ts * 512:c * T + ts * 512 + 512] for c in range(DC)],
                    extra_w=bank_free[bkk] + (extra_w if first else ()))
                first = False
                E("vector", "tensor_copy",
                  out_sb[:, oc * T + ts * 512:oc * T + ts * 512 + 512], Sb(bkk),
                  w=((pe, v),))
                bank_free[bkk] = ((ve, IL("vector", ve)),)

    def proj_tm_256(out_sb, w_sb, act_T, extra_w=()):
        first = True
        for tt in range(NT):
            bkk = tt % 4
            v = mm_group(
                S_ps[:, bkk * 512:bkk * 512 + 256],
                [act_T[:, c * T + tt * 128:c * T + tt * 128 + 128] for c in range(DC)],
                [w_sb[:, c * 256:(c + 1) * 256] for c in range(DC)],
                extra_w=bank_free[bkk] + (extra_w if first else ()))
            first = False
            E("vector", "tensor_copy", out_sb[:, tt * 256:(tt + 1) * 256],
              S_ps[:, bkk * 512:bkk * 512 + 256], w=((pe, v),))
            bank_free[bkk] = ((ve, IL("vector", ve)),)

    def flash(L, scale, causal, qls, k_row, av_rhs, dh, z_dst, z_w):
        if causal:
            two = L > 2048
            passes = [(0, min(L, 2048))] + ([(2048, L - 2048)] if two else [])
        else:
            two = True
            passes = [(0, 2048), (2048, 2048)]
        se_a0 = 0
        for pi, (off, Lp) in enumerate(passes):
            split = Lp > 1024
            qk_inc = []
            for hb in range(2 if split else 1):
                h0 = hb * 1024
                hl_len = min(1024, Lp - h0)
                for j in range(h0 // 512, (h0 + hl_len + 511) // 512):
                    n = min(512, Lp - j * 512)
                    krs = k_row(off + j * 512, n)
                    for ci, (ql, kr) in enumerate(zip(qls, krs)):
                        E("tensor", "matmul", S_ps[:, j * 512:j * 512 + n], ql, kr,
                          start=(ci == 0), stop=(ci == len(qls) - 1),
                          w=fst["s_free"] if (j == 0 and ci == 0) else ())
                qk_inc.append(IL("tensor", pe))
            pe_qk = qk_inc[-1]
            ve_mask = 0
            if causal and off + Lp == L:
                E("vector", "tensor_tensor", S_ps[:, Lp - 128:Lp], S_ps[:, Lp - 128:Lp],
                  mask_sb[:], ALU.add, w=((pe, pe_qk), (sq, sq_mask)))
                ve_mask = IL("vector", ve)
            if split:
                la_len = 1024
                subA = bass.AP(S_ps, 0, [[4 * 512, 128], [64, la_len // 64]])
                E("vector", "tensor_reduce", m1_sb[:], subA, AX.X, ALU.max, negate=True,
                  w=((pe, qk_inc[0]),))
                IL("vector", ve)
                subB = bass.AP(S_ps, 1024, [[4 * 512, 128], [64, (Lp - 1024) // 64]])
                E("vector", "tensor_reduce", nm_sb[:], subB, AX.X, ALU.max, negate=True,
                  w=((pe, pe_qk),) + (((ve, ve_mask),) if ve_mask else ()))
                vv = IL("vector", ve)
                E("vector", "tensor_scalar", nm_sb[:], nm_sb[:], m1_sb[:], scale,
                  ALU.min, ALU.mult, w=((ve, vv),))
                vv = IL("vector", ve)
            else:
                sub = bass.AP(S_ps, 0, [[4 * 512, 128], [64, Lp // 64]])
                E("vector", "tensor_reduce", nm_sb[:], sub, AX.X, ALU.max, negate=True,
                  w=((pe, pe_qk),) + (((ve, ve_mask),) if ve_mask else ()))
                vv = IL("vector", ve)
                E("vector", "tensor_scalar_mul", nm_sb[:], nm_sb[:], scale, w=((ve, vv),))
                vv = IL("vector", ve)
            if pi == 0:
                nm_use = nm_sb
                if two:
                    E("vector", "tensor_copy", nm2_sb[:], nm_sb[:], w=((ve, vv),))
                    E("vector", "tensor_scalar_mul", pm_sb[:], nm_sb[:], -1.0)
                    vv = IL("vector", ve)
            else:
                E("vector", "tensor_tensor", nm2_sb[:], nm2_sb[:], nm_sb[:], ALU.min,
                  w=((ve, vv),))
                vv = IL("vector", ve)
                E("scalar", "activation", a0_sb[:], nm2_sb[:], AF.Exp, bias=pm_sb[:],
                  w=((ve, vv),))
                se_a0 = IL("scalar", se)
                nm_use = nm2_sb
            lpa = l0_sb if pi == 0 else l1_sb
            nla = min(1024, Lp)
            E("scalar", "activation", p_sb[:, 0:nla], S_ps[:, 0:nla], AF.Exp,
              bias=nm_use[:], scale=scale, accum_out=lpa[:],
              w=((ve, vv), (pe, fst["p_pe"])))
            se_expA = IL("scalar", se)
            if split:
                E("scalar", "activation", p_sb[:, 1024:Lp], S_ps[:, 1024:Lp], AF.Exp,
                  bias=nm_use[:], scale=scale, accum_out=l2_sb[:])
                se_expB = IL("scalar", se)
                E("vector", "tensor_add", lpa[:], lpa[:], l2_sb[:], w=((se, se_expB),))
                IL("vector", ve)
            else:
                se_expB = se_expA
            fst["s_free"] = ((se, se_expB),)
            nch = Lp // 128
            cp_inc = []
            for grp in range(0, nch, 8):
                gend = min(grp + 8, nch)
                half = (grp // 8) % 2
                trb = TRA_ps if half == 0 else TRB_ps
                se_w = se_expA if grp == 0 else se_expB
                for c in range(grp, gend):
                    E("tensor", "transpose", trb[:, (c - grp) * 128:(c - grp + 1) * 128],
                      p_sb[:, c * 128:(c + 1) * 128], id_bf[:],
                      w=(((se, se_w),) + fst["tr"][half]) if c == grp else ())
                pe_tr = IL("tensor", pe)
                E("vector", "tensor_copy", pt_sb[:, grp * 128:gend * 128],
                  trb[:, 0:(gend - grp) * 128],
                  w=((pe, pe_tr),) + (((pe, fst["pt_pe"]),) if grp == 0 else ()))
                cp_inc.append(IL("vector", ve))
                fst["tr"][half] = ((ve, cp_inc[-1]),)
            fst["p_pe"] = pe.n
            kt0 = off // 128
            for c in range(nch):
                wv = ()
                if c == 0:
                    wv = ((ve, cp_inc[0]),) + fst["aux"]
                elif c == 8 and len(cp_inc) > 1:
                    wv = ((ve, cp_inc[1]),)
                E("tensor", "matmul", AUX_ps[:, 0:dh], pt_sb[:, c * 128:(c + 1) * 128],
                  av_rhs(kt0 + c), start=(c == 0), stop=(c == nch - 1), w=wv)
            pe_av = IL("tensor", pe)
            fst["pt_pe"] = pe_av
            if not two:
                E("vector", "reciprocal", rl_sb[:], l0_sb[:], w=((se, se_expA), (ve, ve.n)))
                vv = IL("vector", ve)
                E("vector", "tensor_scalar_mul", z_dst, AUX_ps[:, 0:dh], rl_sb[:],
                  w=((ve, vv), (pe, pe_av)) + z_w)
                fst["aux"] = ((ve, IL("vector", ve)),)
            elif pi == 0:
                E("vector", "tensor_copy", osb_sb[:, 0:dh], AUX_ps[:, 0:dh],
                  w=((pe, pe_av),))
                fst["aux"] = ((ve, IL("vector", ve)),)
            else:
                E("vector", "scalar_tensor_tensor", l0_sb[:], l0_sb[:], a0_sb[:],
                  l1_sb[:], ALU.mult, ALU.add, w=((se, se_a0), (ve, ve.n)))
                vv = IL("vector", ve)
                E("vector", "reciprocal", rl_sb[:], l0_sb[:], w=((ve, vv),))
                vv = IL("vector", ve)
                E("vector", "scalar_tensor_tensor", osb_sb[:, 0:dh], osb_sb[:, 0:dh],
                  a0_sb[:], AUX_ps[:, 0:dh], ALU.mult, ALU.add, w=((ve, vv), (pe, pe_av)))
                fst["aux"] = ((ve, IL("vector", ve)),)
                E("vector", "tensor_scalar_mul", z_dst, osb_sb[:, 0:dh], rl_sb[:],
                  w=((ve, ve.n),) + z_w)
                IL("vector", ve)

    sq_z = 0
    for g in range(_NG):
        wws = []
        fw = fence()
        for w_s, w_dd in ((wk_sb, wkt_d), (wq_sb, wqt_d), (wv_sb, wvt_d)):
            gv = p.dma("gpsimd", w_s[:].rearrange("p (c n) -> p c n", c=DC),
                       w_dd.rearrange("(c p) n -> p c n", p=128)[:, :, g * 256:(g + 1) * 256],
                       gq, w=fw)
            wws.append((gq, gv))
        if g == 0:
            emit_conversions()
            sq_conv[0] = sq.n
        for i in range(4):
            bank_free[i] = bank_free[i] + fst["s_free"]
        proj_fm(kt_g[:], 2, wk_sb[:], 256, xn_T[:], extra_w=(wws[0],))
        proj_fm(qt_g[:], 2, wq_sb[:], 256, xn_T[:], extra_w=(wws[1],))
        proj_tm_256(v_g[:], wv_sb[:], xn_T[:], extra_w=(wws[2],))
        fst["s_free"] = bank_free[0] + bank_free[1] + bank_free[2] + bank_free[3] + \
            fst["s_free"]
        for hl in range(_NH):
            pl, pb = hl // 2, 64 * (hl % 2)
            for qt in range(_Q0, min(NT, _NQ)):
                flash(L=(qt + 1) * 128, scale=0.125, causal=True,
                      qls=[qt_g[pb:pb + 64, pl * T + qt * 128:pl * T + qt * 128 + 128]],
                      k_row=lambda off, n, pl=pl, pb=pb:
                          [kt_g[pb:pb + 64, pl * T + off:pl * T + off + n]],
                      av_rhs=lambda kt, hl=hl:
                          v_g[:, kt * 256 + hl * 64:kt * 256 + hl * 64 + 64],
                      dh=64,
                      z_dst=zg_sb[:, qt * 256 + hl * 64:qt * 256 + hl * 64 + 64],
                      z_w=((sq, sq_z),))
        dstz = z_h.rearrange("(t p) n -> p t n", p=128)[:, :, g * 256:(g + 1) * 256]
        sq_z = p.dma("sync", dstz, zg_sb[:].rearrange("p (t n) -> p t n", n=256), sq,
                     w=((ve, ve.n),))

    if stage == "z":
        p.dma("gpsimd", out_d, z_h, gq, w=((sq, sq_z),))
        p.emit("gpsimd", "nop", w=((gq, gq.n),))
        return _close(nc, p)

    gv_wot = p.dma("gpsimd", wot_sb[:].rearrange("p (c n) -> p c n", c=DC),
                   wot_d.rearrange("(c p) n -> p c n", p=128), gq, w=fence())
    for i in range(4):
        bank_free[i] = bank_free[i] + fst["s_free"]
    zt_free = [(), ()]
    sst_free = [(), ()]
    for tt in range(NT):
        b = tt % 2
        zt = zt_sb[:, b * KB:(b + 1) * KB]
        zT = zT_sb[:, b * KB:(b + 1) * KB]
        sst = sst_sb[:, b * KB:(b + 1) * KB]
        zv = p.dma("sync", zt, z_h.rearrange("(t p) n -> p t n", p=128)[:, tt, :], sq,
                   w=zt_free[b] + ((sq, sq_z),))
        for c in range(DC):
            E("tensor", "transpose", TR_ps[:, c * 128:(c + 1) * 128],
              zt[:, c * 128:(c + 1) * 128], id_bf[:],
              w=(((sq, zv),) + fst["tr"][0] + fst["tr"][1]) if c == 0 else ())
        pe_tr = IL("tensor", pe)
        zt_free[b] = ((pe, pe_tr),)
        E("vector", "tensor_copy", zT[:].rearrange("p (c n) -> p c n", n=128),
          TR_ps[:].rearrange("p (c n) -> p c n", n=128), w=((pe, pe_tr),))
        ve_zT = IL("vector", ve)
        fst["tr"] = [((ve, ve_zT),), ((ve, ve_zT),)]
        for half in range(2):
            bk = (tt % 2) * 2 + half
            v = mm_group(Sb(bk),
                         [zT[:, c * 128:(c + 1) * 128] for c in range(DC)],
                         [wot_sb[:, c * KB + half * 512:c * KB + half * 512 + 512]
                          for c in range(DC)],
                         extra_w=((ve, ve_zT), (gq, gv_wot)) + bank_free[bk])
            E("vector", "tensor_copy", sst[:, half * 512:half * 512 + 512], Sb(bk),
              w=((pe, v),) + (sst_free[b] if half == 0 else ()))
            bank_free[bk] = ((ve, IL("vector", ve)),)
        sv = p.dma("sync", s_h[tt * 128:(tt + 1) * 128, :], sst, sq, w=((ve, ve.n),))
        sst_free[b] = ((sq, sv),)
    sq_p1c = sq.n

    if stage == "s":
        __fence = ((pe, pe.n), (ve, ve.n), (se, se.n))
        _copy_out(p, sq, s_h, out_d, xls_sb, sq_p1c, extra=__fence)
        return _close(nc, p)

    sqf_free = [(), ()]
    aux_free = fst["aux"]
    for ts in range(T // 512):
        for c in range(DC):
            b = c % 2
            sqf = sqf_sb[:, b * 512:(b + 1) * 512]
            E("scalar", "activation", sqf,
              xn_T[:, c * T + ts * 512:c * T + ts * 512 + 512], AF.Square,
              w=sqf_free[b])
            se_sq = IL("scalar", se)
            E("tensor", "matmul", AUX_ps[0:1, 0:512], onesc_sb[:], sqf,
              start=(c == 0), stop=(c == DC - 1),
              w=((se, se_sq), (ve, ve_onesc)) + (aux_free if c == 0 else ()))
            sqf_free[b] = ((pe, IL("tensor", pe)),)
        pe_ss = pe.n
        rrow = rrow_sb[:, (ts % 2) * 512:(ts % 2) * 512 + 512]
        E("scalar", "activation", rrow, AUX_ps[0:1, 0:512], AF.Ln, scale=1.0 / D,
          bias=eps6_sb[0:1, :], w=((pe, pe_ss), (ve, ve_onesc)))
        v = IL("scalar", se)
        E("scalar", "activation", rrow, rrow, AF.Exp, scale=-0.5, w=((se, v),))
        se_r = IL("scalar", se)
        aux_free = ((se, se_r),)
        bk = ts % 2
        E("tensor", "matmul", Sb(bk), onesr_sb[:], rrow, start=True, stop=True,
          w=((se, se_r),) + bank_free[bk])
        pe_bc = IL("tensor", pe)
        for c in range(DC):
            E("vector", "scalar_tensor_tensor",
              nx_T[:, c * T + ts * 512:c * T + ts * 512 + 512],
              xn_T[:, c * T + ts * 512:c * T + ts * 512 + 512],
              g1_sb[:, c:c + 1], Sb(bk), ALU.mult, ALU.mult,
              w=((pe, pe_bc), (sq, sq_g1)) if c == 0 else ())
        bank_free[bk] = ((ve, IL("vector", ve)),)
    fst["aux"] = aux_free
    ve_f1 = ve.n

    if stage == "f1":
        __fence = ((pe, pe.n), (ve, ve.n), (se, se.n))
        _copy_out(p, sq, xn_h, out_d, xls_sb, sq_p1c, extra=__fence)
        return _close(nc, p)

    gv_fwq = p.dma("gpsimd", fwq_sb[:].rearrange("p (c n) -> p c n", c=DC),
                   fwqt_d.rearrange("(c p) n -> p c n", p=128), gq, w=fence())
    gv_fwkc = p.dma("gpsimd", fwkc_sb[:].rearrange("p (c n) -> p c n", c=DC),
                    fwkct_d.rearrange("(c p) n -> p c n", p=128), gq)
    gv_fwvc = p.dma("gpsimd", fwvc_sb[:].rearrange("p (c n) -> p c n", c=DC),
                    fwvct_d.rearrange("(c p) n -> p c n", p=128), gq)
    proj_fm(qt_f[:], 8, fwq_sb[:], KB, nx_T[:], extra_w=((gq, gv_fwq), (ve, ve_f1)))
    proj_fm(kct_sb[:], 2, fwkc_sb[:], 256, nx_T[:],
            extra_w=((gq, gv_fwkc), (sq, sq_conv[0])))
    proj_tm_256(vc_sb[:], fwvc_sb[:], nx_T[:], extra_w=((gq, gv_fwvc),))
    fst["s_free"] = bank_free[0] + bank_free[1] + bank_free[2] + bank_free[3] + \
        fst["s_free"]

    if stage == "f2":
        __fence = ((pe, pe.n), (ve, ve.n), (se, se.n))
        _copy_out(p, sq, xn_h, out_d, xls_sb, sq_p1c, extra=__fence)
        return _close(nc, p)

    for hl in range(4):
        for qt in range(NT):
            flash(L=T, scale=1.0 / 16.0, causal=False,
                  qls=[qt_f[:, (2 * hl + j) * T + qt * 128:(2 * hl + j) * T + qt * 128 + 128]
                       for j in range(2)],
                  k_row=lambda off, n: [kct_sb[:, j * T + off:j * T + off + n]
                                        for j in range(2)],
                  av_rhs=lambda kt: vc_sb[:, kt * 256:(kt + 1) * 256],
                  dh=256,
                  z_dst=zf_sb[:, qt * KB + hl * 256:qt * KB + hl * 256 + 256],
                  z_w=())
    ve_f2b = ve.n
    pe_f2b = pe.n

    if stage == "f2b":
        __fence = ((pe, pe.n), (ve, ve.n), (se, se.n))
        _copy_out(p, sq, xn_h, out_d, xls_sb, sq_p1c, extra=__fence)
        return _close(nc, p)

    gv_fwot = p.dma("gpsimd", fwot_sb[:].rearrange("p (c n) -> p c n", c=DC),
                    fwot_d.rearrange("(c p) n -> p c n", p=128), gq, w=fence())
    for i in range(4):
        bank_free[i] = bank_free[i] + fst["s_free"]
    zft_free = [(), ()]
    x1_free = [(), ()]
    xn_free = [(), ()]
    xff_free = [(), ()]
    aux_r = fst["aux"]
    b23_free = ()
    for tt in range(NT):
        b = tt % 2
        zft = zft_sb[:, b * KB:(b + 1) * KB]
        zfT = zfT_sb[:, b * KB:(b + 1) * KB]
        xnl = xnl_sb[:, b * KB:(b + 1) * KB]
        x1t = x1t_sb[:, b * KB:(b + 1) * KB]
        xff = xff_sb[:, b * KB:(b + 1) * KB]
        for c in range(DC):
            E("tensor", "transpose", TR_ps[:, c * 128:(c + 1) * 128],
              zf_sb[:, tt * KB + c * 128:tt * KB + (c + 1) * 128], id_bf[:],
              w=(((ve, ve_f2b),) + fst["tr"][0] + fst["tr"][1]) if c == 0 else ())
        pe_tr = IL("tensor", pe)
        E("vector", "tensor_copy", zfT[:].rearrange("p (c n) -> p c n", n=128),
          TR_ps[:].rearrange("p (c n) -> p c n", n=128), w=((pe, pe_tr),))
        ve_zfT = IL("vector", ve)
        fst["tr"] = [((ve, ve_zfT),), ((ve, ve_zfT),)]
        for half in range(2):
            mm_group(Sb(half),
                     [zfT[:, c * 128:(c + 1) * 128] for c in range(DC)],
                     [fwot_sb[:, c * KB + half * 512:c * KB + half * 512 + 512]
                      for c in range(DC)],
                     extra_w=((ve, ve_zfT), (gq, gv_fwot)) + bank_free[half])
        pe_ao = pe.n
        xnv = p.dma("sync", xnl, xn_h[tt * 128:(tt + 1) * 128, :], sq,
                    w=xn_free[b] + ((sq, sq_p0), (pe, pe_f2b)))
        E("vector", "tensor_add", x1t[:, 0:512], xnl[:, 0:512], Sb(0),
          w=((pe, pe_ao), (sq, xnv)) + x1_free[b])
        E("vector", "tensor_add", x1t[:, 512:1024], xnl[:, 512:1024], Sb(1))
        ve_x1 = IL("vector", ve)
        xn_free[b] = ((ve, ve_x1),)
        bank_free[0] = ((ve, ve_x1),)
        bank_free[1] = ((ve, ve_x1),)
        x1v = p.dma("sync", x1_h[tt * 128:(tt + 1) * 128, :], x1t, sq, w=((ve, ve_x1),))
        if "norm2" in _F2C_SKIP:
            continue
        E("scalar", "activation", sqd_sb[:], x1t, AF.Square, accum_out=ssum_sb[:],
          w=((ve, ve_x1),))
        v = IL("scalar", se)
        E("scalar", "activation", ssum_sb[:], ssum_sb[:], AF.Ln, scale=1.0 / D,
          bias=eps6_sb[:], w=((se, v), (ve, ve_onesc)))
        v = IL("scalar", se)
        E("scalar", "activation", rl_sb[:], ssum_sb[:], AF.Exp, scale=-0.5, w=((se, v),))
        se_r2 = IL("scalar", se)
        E("vector", "scalar_tensor_tensor", xff, x1t, rl_sb[:], g2b_sb[:],
          ALU.mult, ALU.mult, w=((se, se_r2), (ve, ve_bcast)) + xff_free[b])
        ve_xf = IL("vector", ve)
        x1_free[b] = ((sq, x1v), (ve, ve_xf))
        if "xft" in _F2C_SKIP:
            continue
        for c in range(DC):
            E("tensor", "transpose", S_ps[:, 1024 + c * 128:1024 + c * 128 + 128],
              xff[:, c * 128:(c + 1) * 128], idf_sb[:],
              w=(((ve, ve_xf), (sq, sq_idf)) + bank_free[2] + bank_free[3] + b23_free)
                if c == 0 else ())
        pe_xt = IL("tensor", pe)
        xff_free[b] = ((pe, pe_xt),)
        if "xfc" in _F2C_SKIP:
            E("vector", "engine_nop", w=((pe, pe_xt),))
            ve_xfT = IL("vector", ve)
            E("scalar", "nop", w=((pe, pe_xt),))
            se_xfT = IL("scalar", se)
            b23_free = ((ve, ve_xfT), (se, se_xfT))
            bank_free[2] = ((ve, ve_xfT),)
            bank_free[3] = ((se, se_xfT),)
            continue
        E("vector", "tensor_copy", xfTf_sb[:].rearrange("p (c n) -> p c n", n=128),
          S_ps[:, 1024:2048].rearrange("p (c n) -> p c n", n=128), w=((pe, pe_xt),))
        ve_xfT = IL("vector", ve)
        E("scalar", "activation",
          xf_T[:].rearrange("p (c n) -> p c n", n=T)[:, :, tt * 128:(tt + 1) * 128],
          S_ps[:, 1024:2048].rearrange("p (c n) -> p c n", n=128), AF.Copy,
          w=((pe, pe_xt), (ve, ve_xfT)))
        se_xfT = IL("scalar", se)
        b23_free = ((ve, ve_xfT), (se, se_xfT))
        bank_free[2] = ((ve, ve_xfT),)
        bank_free[3] = ((se, se_xfT),)
        if "router" in _F2C_SKIP:
            continue
        v = mm_group(AUX_ps[:, 0:8],
                     [xfTf_sb[:, c * 128:(c + 1) * 128] for c in range(DC)],
                     [wrt_sb[:, c * 8:(c + 1) * 8] for c in range(DC)],
                     extra_w=((ve, ve_xfT), (sq, sq_wrt)) + aux_r)
        E("vector", "tensor_add", lg_sb[:], AUX_ps[:, 0:8], ebb_sb[:],
          w=((pe, v), (ve, ve_bcast)))
        vv = IL("vector", ve)
        aux_r = ((ve, vv),)
        E("vector", "tensor_reduce", m1_sb[:], lg_sb[:], AX.X, ALU.max, negate=True,
          w=((ve, vv),))
        vv = IL("vector", ve)
        E("scalar", "activation", rwu_sb[:], lg_sb[:], AF.Exp, bias=m1_sb[:],
          accum_out=den_sb[:], w=((ve, vv),))
        sv = IL("scalar", se)
        E("vector", "reciprocal", den_sb[:], den_sb[:], w=((se, sv),))
        vv = IL("vector", ve)
        E("vector", "tensor_scalar_mul", rw_sb[:], rwu_sb[:], den_sb[:], w=((ve, vv),))
        vv = IL("vector", ve)
        E("vector", "tensor_reduce", m1_sb[:], rw_sb[:], AX.X, ALU.max, w=((ve, vv),))
        vv = IL("vector", ve)
        E("vector", "tensor_single_scalar", eq1_sb[:], rw_sb[:], m1_sb[:],
          ALU.is_equal, w=((ve, vv),))
        vv = IL("vector", ve)
        E("vector", "scalar_tensor_tensor", rw2_sb[:], eq1_sb[:], -2.0, rw_sb[:],
          ALU.mult, ALU.add, w=((ve, vv),))
        vv = IL("vector", ve)
        E("vector", "tensor_reduce", m2_sb[:], rw2_sb[:], AX.X, ALU.max, w=((ve, vv),))
        vv = IL("vector", ve)
        E("vector", "tensor_single_scalar", eq2_sb[:], rw2_sb[:], m2_sb[:],
          ALU.is_equal, w=((ve, vv),))
        E("vector", "tensor_copy", e12_sb[:, 0:1], m1_sb[:])
        E("vector", "tensor_copy", e12_sb[:, 1:2], m2_sb[:])
        vv = IL("vector", ve)
        E("scalar", "activation", e12_sb[:], e12_sb[:], AF.Exp, w=((ve, vv),))
        sv = IL("scalar", se)
        E("vector", "reduce_sum", den_sb[:], e12_sb[:], AX.X, w=((se, sv),))
        vv = IL("vector", ve)
        E("vector", "reciprocal", den_sb[:], den_sb[:], w=((ve, vv),))
        vv = IL("vector", ve)
        E("vector", "tensor_scalar_mul", w12_sb[:], e12_sb[:], den_sb[:], w=((ve, vv),))
        vv = IL("vector", ve)
        E("vector", "tensor_scalar_mul", eq1_sb[:], eq1_sb[:], w12_sb[:, 0:1],
          w=((ve, vv),))
        vv = IL("vector", ve)
        E("vector", "scalar_tensor_tensor", cmb_sb[:, tt * 8:(tt + 1) * 8],
          eq2_sb[:], w12_sb[:, 1:2], eq1_sb[:], ALU.mult, ALU.add, w=((ve, vv),))
        IL("vector", ve)
    sq_f2c = sq.n
    se_f2c = se.n

    if stage == "x1":
        __fence = ((pe, pe.n), (ve, ve.n), (se, se.n))
        _copy_out(p, sq, x1_h, out_d, xls_sb, sq_f2c, extra=__fence)
        return _close(nc, p)

    NSLAB = 4
    SL = T // NSLAB
    pe_f2c = pe.n
    fw3 = fence()
    ring_free = [fw3, fw3, fw3]
    hb_free = {i: bank_free[i] for i in range(4)}
    eob_free = {0: fst["aux"], 1: ()}
    tmps_free = ()
    moe_free = ((se, se_f2c),)
    prev_ve_h = 0
    prev_pe_eo = 0
    for s in range(NSLAB):
        ve_moez = 0
        for e in range(9):
            mi = (3 + e * 3) if e < 8 else 0
            wvs = []
            for k in range(3):
                wvs.append(p.dma("gpsimd", wring[:, k * 8 * KB:(k + 1) * 8 * KB],
                                 ewbf_h[mi + k], gq, w=ring_free[k]))
            w1b = wring[:, 0:8 * KB]
            w3b = wring[:, 8 * KB:16 * KB]
            w2b = wring[:, 16 * KB:24 * KB]
            for fc in range(DC):
                for th in range(2):
                    bk = (fc * 2 + th) % 2
                    v = mm_group(
                        Sb(bk),
                        [w3b[:, c * KB + fc * 128:c * KB + fc * 128 + 128]
                         for c in range(DC)],
                        [xf_T[:, c * T + s * SL + th * 512:c * T + s * SL + th * 512 + 512]
                         for c in range(DC)],
                        extra_w=hb_free[bk] + ((gq, wvs[1]), (se, se_f2c),
                                               (ve, prev_ve_h)))
                    E("scalar", "activation",
                      h3_sb[:, fc * KB + th * 512:fc * KB + th * 512 + 512], Sb(bk),
                      AF.Copy, w=((pe, v),))
                    hb_free[bk] = ((se, IL("scalar", se)),)
            se_h3 = se.n
            for fc in range(DC):
                for th in range(2):
                    bk = 2 + (fc * 2 + th) % 2
                    v = mm_group(
                        Sb(bk),
                        [w1b[:, c * KB + fc * 128:c * KB + fc * 128 + 128]
                         for c in range(DC)],
                        [xf_T[:, c * T + s * SL + th * 512:c * T + s * SL + th * 512 + 512]
                         for c in range(DC)],
                        extra_w=hb_free[bk] + ((gq, wvs[0]),))
                    E("scalar", "activation", tmps_sb[:], Sb(bk), AF.Silu,
                      w=((pe, v),) + tmps_free)
                    sv = IL("scalar", se)
                    E("vector", "tensor_mul",
                      h_sb[:, fc * KB + th * 512:fc * KB + th * 512 + 512],
                      tmps_sb[:], h3_sb[:, fc * KB + th * 512:fc * KB + th * 512 + 512],
                      w=((se, sv), (se, se_h3), (pe, prev_pe_eo)))
                    vv = IL("vector", ve)
                    hb_free[bk] = ((ve, vv),)
                    tmps_free = ((ve, vv),)
            ve_h = ve.n
            pe_h1 = pe.n
            for tl in range(SL // 128):
                for dhh in range(2):
                    eops = AUX_ps if dhh == 0 else SPR_ps
                    v = mm_group(
                        eops[:, 0:512],
                        [h_sb[:, fc * KB + tl * 128:fc * KB + tl * 128 + 128]
                         for fc in range(DC)],
                        [w2b[:, fc * KB + dhh * 512:fc * KB + dhh * 512 + 512]
                         for fc in range(DC)],
                        extra_w=((ve, ve_h), (gq, wvs[2])) + eob_free[dhh])
                    dst = moe_sb[:, tl * KB + dhh * 512:tl * KB + dhh * 512 + 512]
                    scl = cmb_sb[:, (s * 8 + tl) * 8 + e:(s * 8 + tl) * 8 + e + 1] \
                        if e < 8 else 1.0
                    if e == 0:
                        E("vector", "tensor_scalar_mul", dst, eops[:, 0:512], scl,
                          w=((pe, v),) + moe_free)
                    else:
                        E("vector", "scalar_tensor_tensor", dst, eops[:, 0:512], scl,
                          dst, ALU.mult, ALU.add, w=((pe, v),))
                    eob_free[dhh] = ((ve, IL("vector", ve)),)
            ring_free[0] = ((pe, pe_h1),)
            ring_free[1] = ((pe, pe_h1),)
            ring_free[2] = ((pe, pe.n),)
            prev_ve_h = ve_h
            prev_pe_eo = pe.n
        ve_moe = ve.n
        xl_free = sl_free = x1l_free = ()
        prev_store = ()
        for tl in range(SL // 128):
            tt = s * (SL // 128) + tl
            xv = p.dma("sync", xls_sb[:], x_d[tt * 128:(tt + 1) * 128, :], sq,
                       w=xl_free + ((pe, pe_f2c),))
            sv = p.dma("sync", sls_sb[:], s_h[tt * 128:(tt + 1) * 128, :], sq,
                       w=sl_free + ((sq, sq_p1c),))
            x1v = p.dma("sync", x1l_sb[:], x1_h[tt * 128:(tt + 1) * 128, :], sq,
                        w=x1l_free + ((sq, sq_f2c),))
            E("vector", "tensor_add", gl_sb[:], x1l_sb[:],
              moe_sb[:, tl * KB:(tl + 1) * KB], w=((sq, x1v), (ve, ve_moe)))
            x1l_free = ((ve, IL("vector", ve)),)
            E("scalar", "activation", gl_sb[:], gl_sb[:], AF.Silu, w=((ve, ve.n),))
            sv2 = IL("scalar", se)
            E("vector", "tensor_mul", gl_sb[:], gl_sb[:], sls_sb[:],
              w=((se, sv2), (sq, sv)))
            sl_free = ((ve, IL("vector", ve)),)
            E("vector", "tensor_add", outt_sb[:], gl_sb[:], xls_sb[:],
              w=((ve, ve.n), (sq, xv)) + prev_store)
            xl_free = ((ve, IL("vector", ve)),)
            ov = p.dma("sync", out_d[tt * 128:(tt + 1) * 128, :], outt_sb[:], sq,
                       w=((ve, ve.n),))
            prev_store = ((sq, ov),)
        moe_free = ((ve, ve.n),)

    return _close(nc, p)


def _copy_out(p, sq, src_h, out_d, tile_sb, sq_src, extra=()):
    last = ((sq, sq_src),) + extra
    for tt in range(NT):
        v = p.dma("sync", tile_sb[:], src_h[tt * 128:(tt + 1) * 128, :], sq, w=last)
        ov = p.dma("sync", out_d[tt * 128:(tt + 1) * 128, :], tile_sb[:], sq,
                   w=((sq, v),))
        last = ((sq, ov),)


def _close(nc, p):
    sems = {s.name: s for s in p.sems}
    p.emit("sync", "nop", w=((sems["sq"], sems["sq"].n), (sems["gq"], sems["gq"].n)))
    p.emit("gpsimd", "nop", w=((sems["gq"], sems["gq"].n),))
    with contextlib.ExitStack() as stack:
        handles = {}
        for s in p.sems:
            handles[s] = stack.enter_context(nc.semaphore(s.name))
        block = stack.enter_context(nc.Block())
        for eng in Prog.ENGINES:
            def make(eng_name):
                def f(eng_obj):
                    for meth, args, kw, waits, incs in p.ops[eng_name]:
                        for sm, v in waits:
                            eng_obj.wait_ge(handles[sm], v)
                        ins = getattr(eng_obj, meth)(*args, **kw)
                        for sm, k in incs:
                            ins = ins.then_inc(handles[sm], k)
                return f
            getattr(block, eng)(make(eng))
    return nc


def _prep_inputs(inp):
    f32 = lambda a: np.ascontiguousarray(np.asarray(a, dtype=np.float32))
    tr = lambda a: np.ascontiguousarray(np.asarray(a, dtype=np.float32).T)
    ews = np.empty((27, D, D), np.float32)
    ews[0] = np.asarray(inp["L0_ffn_sW1"], np.float32).T
    ews[1] = np.asarray(inp["L0_ffn_sW3"], np.float32).T
    ews[2] = np.asarray(inp["L0_ffn_sW2"], np.float32).T
    eW1 = np.asarray(inp["L0_ffn_eW1"], np.float32)
    eW2 = np.asarray(inp["L0_ffn_eW2"], np.float32)
    eW3 = np.asarray(inp["L0_ffn_eW3"], np.float32)
    for e in range(8):
        ews[3 + 3 * e + 0] = eW1[e].T
        ews[3 + 3 * e + 1] = eW3[e].T
        ews[3 + 3 * e + 2] = eW2[e].T
    mask = np.where(np.arange(128)[None, :] > np.arange(128)[:, None],
                    np.float32(-1e10), np.float32(0.0)).astype(np.float32)
    common = dict(
        wkt=tr(np.asarray(inp["L0_attn_W_K"], np.float32).reshape(D, D)),
        wqt=tr(np.asarray(inp["L0_attn_W_Q"], np.float32).reshape(D, D)),
        wvt=tr(np.asarray(inp["L0_attn_W_V"], np.float32).reshape(D, D)),
        wot=tr(inp["L0_attn_W_O"]),
        fwqt=tr(inp["L0_ffn_Wq"]), fwkct=tr(inp["L0_ffn_Wkc"]),
        fwvct=tr(inp["L0_ffn_Wvc"]), fwot=tr(inp["L0_ffn_Wo"]),
        fwrt=tr(inp["L0_ffn_Wr"]), ews=np.ascontiguousarray(ews),
        g1=f32(inp["L0_ffn_g1"]), g2=f32(inp["L0_ffn_g2"]),
        eb=f32(inp["L0_ffn_expert_bias"]),
        ident=np.eye(128, dtype=np.float32), mask=np.ascontiguousarray(mask),
        ones=np.ones((1, 128), np.float32),
    )
    x = np.asarray(inp["L0_attn_x"], np.float32)
    return [dict(common, x=np.ascontiguousarray(x[b])) for b in range(8)]


def kernel(**inputs):
    in_maps = _prep_inputs(inputs)
    nc = build(_STAGE)
    res = run_bass_kernel_spmd(nc, in_maps, core_ids=list(range(8)))
    return np.stack([np.asarray(r["out"]) for r in res.results], axis=0)
